# version 46
# baseline (speedup 1.0000x reference)
import numpy as np
import ml_dtypes
from contextlib import ExitStack, contextmanager

import concourse.bass as bass
import concourse.mybir as mybir
from concourse.bass_utils import run_bass_kernel_spmd

F32 = mybir.dt.float32
BF16 = mybir.dt.bfloat16
AF = mybir.ActivationFunctionType
ALU = mybir.AluOpType
AX = mybir.AxisListType

NCORES = 8
T = 2048
D = 2048
NTC = 16
NH, NKV, HD, PD = 16, 4, 128, 16
KD = NKV * HD
H1 = 4096
EPS_RMS = float(np.finfo(np.float32).eps)
EPS_LN = 1e-5
RSQRT_HD = 1.0 / float(np.sqrt(HD))
NEG_BIG = -1.0e30


class _Op:
    __slots__ = ("emit", "dma", "deps", "needed", "ms", "dma_ord")

    def __init__(self, dma, deps):
        self.emit = None
        self.dma = dma
        self.deps = deps
        self.needed = False
        self.ms = 0
        self.dma_ord = -1


class KB:
    STREAMS = ("pe", "act", "dve", "pool", "sync")
    ENG_OF = dict(pe="tensor", act="scalar", dve="vector", pool="gpsimd", sync="sync")
    NR = 16

    def __init__(self, nc):
        self.nc = nc
        self.mode = "plan"
        self.ops = {s: [] for s in self.STREAMS}
        self.pos = {s: 0 for s in self.STREAMS}
        self.dma_count = {s: 0 for s in self.STREAMS}
        self.buf = {}
        self.T = {}
        self.scope_stacks = {}
        self.scope_tiles = {}
        self.fence_due = {s: [] for s in self.STREAMS}
        self.highwater = {s: {} for s in self.STREAMS}
        self.sems = {}
        self.sbuf_bytes = {}

    def sbuf(self, name, shape, dtype, scope):
        if self.mode == "plan":
            esz = 2 if dtype == BF16 else 4
            self.sbuf_bytes[scope] = self.sbuf_bytes.get(scope, 0) + shape[-1] * esz
            self.scope_tiles.setdefault(scope, []).append(name)
            return name
        t = self.scope_stacks[scope].enter_context(
            self.nc.sbuf_tensor(f"{name}_{scope}", list(shape), dtype))
        self.T[name] = t
        return name

    def t(self, name):
        return self.T[name]

    def open_scope(self, name):
        if self.mode == "plan":
            self.scope_tiles.setdefault(name, [])
        else:
            self.scope_stacks[name] = ExitStack()

    def close_scope(self, name):
        if self.mode == "plan":
            handles = set()
            names = set(self.scope_tiles.get(name, ()))
            for key, ent in self.buf.items():
                if key[0] in names:
                    if ent[0] is not None:
                        handles.add(ent[0])
                    handles.update(ent[1])
            for s in self.STREAMS:
                self.fence_due[s].extend(handles)
        else:
            self.scope_stacks[name].close()
            del self.scope_stacks[name]

    def op(self, stream, emit, reads=(), writes=(), dma=False):
        if self.mode == "plan":
            handle = (stream, len(self.ops[stream]))
            deps = {}

            def add(h):
                if h is None:
                    return
                s, i = h
                if s == stream:
                    if s == "pe":
                        return
                    if dma and self.ops[s][i].dma:
                        return
                    if (not dma and not self.ops[s][i].dma
                            and len(self.ops[s]) - i > 4):
                        return
                deps[(s, i)] = True

            for h in self.fence_due[stream]:
                add(h)
            self.fence_due[stream] = []
            for k in reads:
                ent = self.buf.get(k)
                if ent is not None and ent[0] is not None:
                    add(ent[0])
            for k in writes:
                ent = self.buf.get(k)
                if ent is not None:
                    if ent[0] is not None:
                        add(ent[0])
                    for r in ent[1]:
                        add(r)
            best = {}
            for (s, i) in deps:
                o = self.ops[s][i]
                kd = (s, "d", (o.dma_ord - 1) % self.NR) if o.dma else (s, "c")
                if kd not in best or best[kd] < i:
                    best[kd] = i
            dl = sorted({(k[0], v) for k, v in best.items()})
            rec = _Op(dma, dl)
            if dma:
                self.dma_count[stream] += 1
                rec.dma_ord = self.dma_count[stream]
            self.ops[stream].append(rec)
            for (s, i) in dl:
                tgt = self.ops[s][i]
                if not tgt.dma:
                    tgt.needed = True
            for k in reads:
                self.buf.setdefault(k, [None, []])[1].append(handle)
            for k in writes:
                self.buf[k] = [handle, []]
            return handle
        else:
            i = self.pos[stream]
            self.pos[stream] += 1
            rec = self.ops[stream][i]
            assert rec.dma == dma, f"plan/emit divergence on {stream}[{i}]"
            rec.emit = emit
            return (stream, i)

    def finish_plan(self):
        for s in self.STREAMS:
            ms = 0
            ordn = 0
            for rec in self.ops[s]:
                if rec.dma:
                    ordn += 1
                    rec.dma_ord = ordn
                elif rec.needed:
                    ms += 1
                    rec.ms = ms
        self.buf = {}
        self.fence_due = {s: [] for s in self.STREAMS}
        self.mode = "emit"

    def make_sems(self, stack):
        for s in self.STREAMS:
            self.sems["s" + s] = stack.enter_context(self.nc.semaphore("s" + s))
            for r in range(self.NR):
                self.sems[f"q{s}_{r}"] = stack.enter_context(
                    self.nc.semaphore(f"q{s}_{r}"))

    def waitval(self, handle):
        s, i = handle
        rec = self.ops[s][i]
        if rec.dma:
            o = rec.dma_ord - 1
            return (f"q{s}_{o % self.NR}", 16 * (o // self.NR + 1))
        return ("s" + s, rec.ms)

    def totals(self):
        out = {}
        for s in self.STREAMS:
            nd = sum(1 for r in self.ops[s] if r.dma)
            nm = max([r.ms for r in self.ops[s] if not r.dma], default=0)
            for r in range(self.NR):
                cnt = (nd - r + self.NR - 1) // self.NR if nd > r else 0
                out[f"q{s}_{r}"] = 16 * cnt
            out["s" + s] = nm
        return out

    @contextmanager
    def phase(self, name):
        if self.mode == "plan":
            yield
            return
        start = {s: self.pos[s] for s in self.STREAMS}
        yield
        end = {s: self.pos[s] for s in self.STREAMS}
        with self.nc.Block(name) as blk:
            for s in self.STREAMS:
                if end[s] == start[s]:
                    continue
                dec = getattr(blk, self.ENG_OF[s])

                def body(eng, s=s, a=start[s], b=end[s]):
                    hw = self.highwater[s]
                    for i in range(a, b):
                        rec = self.ops[s][i]
                        for dep in rec.deps:
                            sem, val = self.waitval(dep)
                            if hw.get(sem, 0) < val:
                                eng.wait_ge(self.sems[sem], val)
                                hw[sem] = val
                        inst = rec.emit(eng)
                        if inst is None:
                            continue
                        if rec.dma:
                            inst.then_inc(
                                self.sems[f"q{s}_{(rec.dma_ord - 1) % self.NR}"], 16)
                        elif rec.needed:
                            inst.then_inc(self.sems["s" + s], 1)

                dec(body)


def build_all(kb, dram):
    nc = kb.nc
    t = kb.t
    PS = [f"ps{i}" for i in range(8)]

    def psv(i, a=0, b=512):
        return t(PS[i])[:, a:b]

    def score_geom(qt):
        span = (qt + 1) * 128
        nch = (span + 511) // 512
        widths = [min(512, span - 512 * c) for c in range(nch)]
        off = widths[-1] - 128
        return span, nch, widths, off

    kb.open_scope("glob")
    with kb.phase("setup"):
        for i in range(8):
            if kb.mode == "plan":
                kb.scope_tiles.setdefault("glob", []).append(PS[i])
            else:
                kb.T[PS[i]] = kb.scope_stacks["glob"].enter_context(
                    nc.psum_tensor(PS[i], [128, 512], F32))
        kb.sbuf("ident", [128, 128], BF16, "glob")
        kb.sbuf("maskt", [128, 128], F32, "glob")
        kb.sbuf("maskb", [128, 128], BF16, "glob")
        kb.sbuf("b1c", [128, 32], F32, "glob")
        kb.sbuf("b2c", [128, 32], F32, "glob")
        kb.sbuf("b3c", [128, 16], F32, "glob")
        kb.sbuf("qgsb", [1, 16], F32, "glob")
        kb.sbuf("ones1", [1, 128], F32, "glob")
        names = [("qgs", 16)]
        for p in range(3):
            for h4 in range(4):
                names += [(f"nm{p}_{h4}", 1), (f"zz{p}_{h4}", 1), (f"rz{p}_{h4}", 1),
                          (f"cmx{p}_{h4}", 4), (f"zp{p}_{h4}", 4)]
        for i in range(4):
            names += [(f"bn24_{i}", 24), (f"summ_{i}", 1), (f"sM2_{i}", 1),
                      (f"meant_{i}", 1), (f"dev_{i}", 8), (f"dev2_{i}", 8),
                      (f"sdev_{i}", 1), (f"vart_{i}", 1), (f"lnv_{i}", 1),
                      (f"rstd1_{i}", 1), (f"nmr_{i}", 1)]
        for p in range(2):
            names += [(f"ssq{p}", 16), (f"ssk{p}", 4), (f"rqraw{p}", 16),
                      (f"rkraw{p}", 4), (f"rstdq{p}", 16), (f"lnt{p}", 20)]
        for nm, w in names:
            kb.sbuf(nm, [128, w], F32, "glob")

        kb.op("sync", lambda e: e.dma_start(out=t("ident")[:], in_=dram["identi"]),
              writes=[("ident", 0)], dma=True)
        kb.op("sync", lambda e: e.dma_start(out=t("maskt")[:], in_=dram["maski"]),
              writes=[("maskt", 0)], dma=True)
        kb.op("sync", lambda e: e.dma_start(out=t("maskb")[:], in_=dram["masktb"]),
              writes=[("maskb", 0)], dma=True)
        kb.op("sync", lambda e: e.dma_start(out=t("b1c")[:], in_=dram["b1c"]),
              writes=[("b1c", 0)], dma=True)
        kb.op("sync", lambda e: e.dma_start(out=t("b2c")[:], in_=dram["b2c"]),
              writes=[("b2c", 0)], dma=True)
        kb.op("sync", lambda e: e.dma_start(out=t("b3c")[:], in_=dram["b3c"]),
              writes=[("b3c", 0)], dma=True)
        kb.op("sync", lambda e: e.dma_start(out=t("qgsb")[:], in_=dram["qgi"]),
              writes=[("qgsb", 0)], dma=True)
        kb.op("pool", lambda e: e.memset(t("ones1")[:], 1.0), writes=[("ones1", 0)])
        kb.op("pe", lambda e: e.matmul(psv(7, 0, 16), t("ones1")[:], t("qgsb")[:],
                                       start=True, stop=True),
              reads=[("ones1", 0), ("qgsb", 0)], writes=[("ps", 7)])
        kb.op("dve", lambda e: e.tensor_scalar_mul(t("qgs")[:], psv(7, 0, 16), RSQRT_HD),
              reads=[("ps", 7)], writes=[("qgs", 0)])
        def emit_w_casts():
            for nm, srcn, dstn, parts in (("w1d", "w1b", "w1d", 2), ("w2d", "w2b", "w2d", 4),
                                          ("w3d", "w3b", "w3d", 2)):
                n0 = dram[srcn].shape[0]
                step = n0 // parts
                for p in range(parts):
                    kb.op("pool",
                          lambda e, srcn=srcn, dstn=dstn, p=p, step=step:
                              e.dma_start(out=dram[dstn][p * step:(p + 1) * step],
                                          in_=dram[srcn][p * step:(p + 1) * step]),
                          writes=[(nm, p)], dma=True)
        kb.emit_w_casts = emit_w_casts

    for seq in range(2):
        sc = f"seq{seq}"
        kb.open_scope(sc)
        s1 = f"s1_{seq}"
        kb.open_scope(s1)
        with kb.phase(s1):
            kb.sbuf("kT", [128, NKV * T], BF16, sc)
            kb.sbuf("vres", [128, NTC * KD], BF16, sc)
            kb.sbuf("wq", [128, 16 * D], BF16, s1)
            kb.sbuf("wk", [128, 16 * KD], BF16, s1)
            kb.sbuf("wv", [128, 16 * KD], BF16, s1)
            kb.sbuf("cosb", [128, T], F32, s1)
            kb.sbuf("sinb", [128, T], F32, s1)
            for i in range(2):
                kb.sbuf(f"xt{i}", [128, D], BF16, s1)
                kb.sbuf(f"qtc{i}", [128, D], BF16, s1)
                kb.sbuf(f"qtok{i}", [128, D], F32, s1)
                kb.sbuf(f"qbf{i}", [128, D], BF16, s1)
            kb.sbuf("ktok0", [128, KD], F32, s1)
            kb.sbuf("kbf0", [128, KD], BF16, s1)
            kb.sbuf("kbf1", [128, KD], BF16, s1)
            kb.sbuf("scr0", [128, D], F32, s1)
            kb.sbuf("scrk0", [128, KD], F32, s1)
            kb.sbuf("tm1", [128, 128], F32, s1)
            kb.sbuf("tm2", [128, 128], F32, s1)
            kb.sbuf("tm3", [128, 128], F32, s1)
            kb.sbuf("tm4", [128, 128], F32, s1)

            def xload_s1(tc, seq=seq):
                xt = f"xt{tc % 2}"
                kb.op("pool", lambda e, seq=seq, tc=tc, xt=xt: e.dma_start(
                    out=t(xt)[:], in_=dram["xtb"][seq, tc]),
                    writes=[(xt, 0)], dma=True)

            xload_s1(0)
            xload_s1(1)
            for dc in range(16):
                kb.op("pool", lambda e, dc=dc: e.dma_start(
                    out=t("wq")[:, dc * D:(dc + 1) * D],
                    in_=dram["wqt"][dc * 128:(dc + 1) * 128, :]),
                    writes=[("wq", dc)], dma=True)
                kb.op("pool", lambda e, dc=dc: e.dma_start(
                    out=t("wk")[:, dc * KD:(dc + 1) * KD],
                    in_=dram["wkt"][dc * 128:(dc + 1) * 128, :]),
                    writes=[("wk", dc)], dma=True)
                kb.op("pool", lambda e, dc=dc: e.dma_start(
                    out=t("wv")[:, dc * KD:(dc + 1) * KD],
                    in_=dram["wvt"][dc * 128:(dc + 1) * 128, :]),
                    writes=[("wv", dc)], dma=True)
            kb.op("sync", lambda e: e.dma_start(out=t("cosb")[:], in_=dram["cosb"]),
                  writes=[("cosb", 0)], dma=True)
            kb.op("sync", lambda e: e.dma_start(out=t("sinb")[:], in_=dram["sinb"]),
                  writes=[("sinb", 0)], dma=True)

            def s1_front(tc, seq=seq):
                bi = tc % 2
                xt = f"xt{bi}"
                qtok = f"qtok{bi}"
                ktok = "ktok0"
                qbf = f"qbf{bi}"
                kbf = f"kbf{bi}"
                for dc in range(16):
                    for q in range(4):
                        kb.op("pe", lambda e, xt=xt, dc=dc, q=q: e.matmul(
                            psv(q), t(xt)[:, dc * 128:(dc + 1) * 128],
                            t("wq")[:, dc * D + q * 512: dc * D + (q + 1) * 512],
                            start=(dc == 0), stop=(dc == 15)),
                            reads=[(xt, 0), ("wq", dc)], writes=[("ps", q)])
                    kb.op("pe", lambda e, xt=xt, dc=dc: e.matmul(
                        psv(4), t(xt)[:, dc * 128:(dc + 1) * 128],
                        t("wk")[:, dc * KD:(dc + 1) * KD],
                        start=(dc == 0), stop=(dc == 15)),
                        reads=[(xt, 0), ("wk", dc)], writes=[("ps", 4)])
                    kb.op("pe", lambda e, xt=xt, dc=dc: e.matmul(
                        psv(5), t(xt)[:, dc * 128:(dc + 1) * 128],
                        t("wv")[:, dc * KD:(dc + 1) * KD],
                        start=(dc == 0), stop=(dc == 15)),
                        reads=[(xt, 0), ("wv", dc)], writes=[("ps", 5)])
                for q in range(4):
                    kb.op("act", lambda e, q=q, qtok=qtok: e.activation(
                        t(qtok)[:, q * 512:(q + 1) * 512], psv(q), AF.Copy),
                        reads=[("ps", q)], writes=[(qtok, 0)])
                kb.op("act", lambda e, ktok=ktok: e.activation(t(ktok)[:], psv(4), AF.Copy),
                      reads=[("ps", 4)], writes=[(ktok, 0)])
                kb.op("act", lambda e, tc=tc: e.activation(
                    t("vres")[:, tc * KD:(tc + 1) * KD], psv(5), AF.Copy),
                    reads=[("ps", 5)], writes=[("vres", tc)])
                scr, scrk = "scr0", "scrk0"
                ssq, ssk = f"ssq{bi}", f"ssk{bi}"
                lnt, rqraw, rkraw, rstdq = f"lnt{bi}", f"rqraw{bi}", f"rkraw{bi}", f"rstdq{bi}"
                kb.op("act", lambda e, qtok=qtok, scr=scr: e.activation(t(scr)[:], t(qtok)[:], AF.Square),
                      reads=[(qtok, 0)], writes=[(scr, 0)])
                kb.op("act", lambda e, ktok=ktok, scrk=scrk: e.activation(t(scrk)[:], t(ktok)[:], AF.Square),
                      reads=[(ktok, 0)], writes=[(scrk, 0)])
                kb.op("dve", lambda e, scr=scr, ssq=ssq: e.reduce_sum(
                    t(ssq)[:], t(scr)[:].rearrange("p (h j) -> p h j", j=128), axis=AX.X),
                    reads=[(scr, 0)], writes=[(ssq, 0)])
                kb.op("dve", lambda e, scrk=scrk, ssk=ssk: e.reduce_sum(
                    t(ssk)[:], t(scrk)[:].rearrange("p (h j) -> p h j", j=128), axis=AX.X),
                    reads=[(scrk, 0)], writes=[(ssk, 0)])
                kb.op("dve", lambda e, ssq=ssq: e.tensor_scalar(
                    t(ssq)[:], t(ssq)[:], 1.0 / HD, EPS_RMS, op0=ALU.mult, op1=ALU.add),
                    reads=[(ssq, 0)], writes=[(ssq, 0)])
                kb.op("dve", lambda e, ssk=ssk: e.tensor_scalar(
                    t(ssk)[:], t(ssk)[:], 1.0 / HD, EPS_RMS, op0=ALU.mult, op1=ALU.add),
                    reads=[(ssk, 0)], writes=[(ssk, 0)])
                kb.op("act", lambda e, ssq=ssq, lnt=lnt: e.activation(t(lnt)[:, 0:16], t(ssq)[:], AF.Ln),
                      reads=[(ssq, 0)], writes=[(lnt, 0)])
                kb.op("act", lambda e, ssk=ssk, lnt=lnt: e.activation(t(lnt)[:, 16:20], t(ssk)[:], AF.Ln),
                      reads=[(ssk, 0)], writes=[(lnt, 1)])
                kb.op("dve", lambda e, lnt=lnt: e.tensor_scalar_mul(t(lnt)[:], t(lnt)[:], -0.5),
                      reads=[(lnt, 0), (lnt, 1)], writes=[(lnt, 0), (lnt, 1)])
                kb.op("act", lambda e, lnt=lnt, rqraw=rqraw: e.activation(t(rqraw)[:], t(lnt)[:, 0:16], AF.Exp),
                      reads=[(lnt, 0)], writes=[(rqraw, 0)])
                kb.op("act", lambda e, lnt=lnt, rkraw=rkraw: e.activation(t(rkraw)[:], t(lnt)[:, 16:20], AF.Exp),
                      reads=[(lnt, 1)], writes=[(rkraw, 0)])
                kb.op("dve", lambda e, rqraw=rqraw, rstdq=rstdq: e.tensor_mul(
                    t(rstdq)[:], t(rqraw)[:], t("qgs")[:]),
                    reads=[(rqraw, 0), ("qgs", 0)], writes=[(rstdq, 0)])
                for h in range(16):
                    kb.op("dve", lambda e, h=h, qtok=qtok, rstdq=rstdq: e.tensor_scalar_mul(
                        t(qtok)[:, h * 128:(h + 1) * 128],
                        t(qtok)[:, h * 128:(h + 1) * 128], t(rstdq)[:, h:h + 1]),
                        reads=[(qtok, 0), (rstdq, 0)], writes=[(qtok, 0)])
                for h in range(4):
                    kb.op("dve", lambda e, h=h, ktok=ktok, rkraw=rkraw: e.tensor_scalar_mul(
                        t(ktok)[:, h * 128:(h + 1) * 128],
                        t(ktok)[:, h * 128:(h + 1) * 128], t(rkraw)[:, h:h + 1]),
                        reads=[(ktok, 0), (rkraw, 0)], writes=[(ktok, 0)])

                def rope(name, nh, tc=tc):
                    v1 = lambda: t(name)[:].rearrange("p (h d) -> p h d", d=128)[:, 0:nh, 0:8]
                    v2 = lambda: t(name)[:].rearrange("p (h d) -> p h d", d=128)[:, 0:nh, 8:16]
                    cv = lambda: t("cosb")[:, tc * 128:(tc + 1) * 128].rearrange(
                        "p (h j) -> p h j", j=8)[:, 0:nh, :]
                    sv = lambda: t("sinb")[:, tc * 128:(tc + 1) * 128].rearrange(
                        "p (h j) -> p h j", j=8)[:, 0:nh, :]
                    tv = lambda nm: t(nm)[:].rearrange("p (h j) -> p h j", j=8)[:, 0:nh, :]
                    key = (name, 0)
                    kb.op("dve", lambda e: e.tensor_mul(tv("tm1"), v1(), cv()),
                          reads=[key, ("cosb", 0)], writes=[("tm1", 0)])
                    kb.op("dve", lambda e: e.tensor_mul(tv("tm2"), v2(), sv()),
                          reads=[key, ("sinb", 0)], writes=[("tm2", 0)])
                    kb.op("dve", lambda e: e.tensor_mul(tv("tm3"), v1(), sv()),
                          reads=[key, ("sinb", 0)], writes=[("tm3", 0)])
                    kb.op("dve", lambda e: e.tensor_mul(tv("tm4"), v2(), cv()),
                          reads=[key, ("cosb", 0)], writes=[("tm4", 0)])
                    kb.op("dve", lambda e: e.tensor_add(v1(), tv("tm1"), tv("tm2")),
                          reads=[("tm1", 0), ("tm2", 0)], writes=[key])
                    kb.op("dve", lambda e: e.tensor_sub(v2(), tv("tm3"), tv("tm4")),
                          reads=[("tm3", 0), ("tm4", 0)], writes=[key])

                rope(qtok, 16)
                rope(ktok, 4)
                kb.op("dve", lambda e, qbf=qbf, qtok=qtok: e.tensor_copy(t(qbf)[:], t(qtok)[:]),
                      reads=[(qtok, 0)], writes=[(qbf, 0)])
                kb.op("dve", lambda e, kbf=kbf, ktok=ktok: e.tensor_copy(t(kbf)[:], t(ktok)[:]),
                      reads=[(ktok, 0)], writes=[(kbf, 0)])

            def s1_transposes(tc, seq=seq):
                bi = tc % 2
                qbf = f"qbf{bi}"
                kbf = f"kbf{bi}"
                qtc = f"qtc{bi}"
                for g in range(4):
                    bank = 6 + g % 2
                    for j in range(4):
                        hc = g * 4 + j
                        kb.op("pe", lambda e, hc=hc, bank=bank, j=j, qbf=qbf: e.matmul(
                            psv(bank, j * 128, (j + 1) * 128),
                            t(qbf)[:, hc * 128:(hc + 1) * 128], t("ident")[:],
                            start=True, stop=True),
                            reads=[(qbf, 0), ("ident", 0)], writes=[("ps", bank)])
                    kb.op("dve", lambda e, g=g, bank=bank, qtc=qtc: e.tensor_copy(
                        t(qtc)[:, g * 512:(g + 1) * 512], psv(bank)),
                        reads=[("ps", bank)], writes=[(qtc, 0)])
                for j in range(4):
                    kb.op("pe", lambda e, j=j, kbf=kbf: e.matmul(
                        psv(6, j * 128, (j + 1) * 128),
                        t(kbf)[:, j * 128:(j + 1) * 128], t("ident")[:],
                        start=True, stop=True),
                        reads=[(kbf, 0), ("ident", 0)], writes=[("ps", 6)])
                kb.op("dve", lambda e, tc=tc: e.tensor_copy(
                    t("kT")[:].rearrange("p (k n) -> p k n", k=4)[:, :, tc * 128:(tc + 1) * 128],
                    psv(6).rearrange("p (k n) -> p k n", k=4)),
                    reads=[("ps", 6)], writes=[("kTw", tc)])
                kb.op("act", lambda e, seq=seq, tc=tc, qtc=qtc: e.dma_start(
                    out=dram["qtd"][seq, tc], in_=t(qtc)[:]),
                    reads=[(qtc, 0)], writes=[("qtd", seq * 16 + tc)], dma=True)

            for tc in range(NTC):
                if 1 <= tc < NTC - 1:
                    xload_s1(tc + 1)
                s1_front(tc)
                if tc > 0:
                    s1_transposes(tc - 1)
            s1_transposes(NTC - 1)
        kb.close_scope(s1)

        s2 = f"s2_{seq}"
        kb.open_scope(s2)
        with kb.phase(s2):
            kb.sbuf("wp", [128, 16 * D], BF16, s2)
            for i in range(2):
                kb.sbuf(f"qt{i}", [128, D], BF16, s2)
                kb.sbuf(f"pt4_{i}", [128, 512], BF16, s2)
                kb.sbuf(f"res{i}", [128, D], F32, s2)
                kb.sbuf(f"x1bf{i}", [128, D], BF16, s2)
                kb.sbuf(f"xbf{i}", [128, D], BF16, s2)
            for i in range(3):
                kb.sbuf(f"diag4_{i}", [128, 512], BF16, s2)
                for h4 in range(4):
                    kb.sbuf(f"pbf{i}_{h4}", [128, T], BF16, s2)
            kb.sbuf("ytg", [128, 16 * 256], BF16, s2)

            for dc in range(16):
                kb.op("pool", lambda e, dc=dc: e.dma_start(
                    out=t("wp")[:, dc * D:(dc + 1) * D],
                    in_=dram["wpt"][dc * 128:(dc + 1) * 128, :]),
                    writes=[("wp", dc)], dma=True)
            if seq == 0:
                kb.emit_w_casts()

            def qload(qt, seq=seq):
                qb = f"qt{qt % 2}"
                kb.op("sync", lambda e, seq=seq, qt=qt, qb=qb: e.dma_start(
                    out=t(qb)[:], in_=dram["qtd"][seq, qt]),
                    reads=[("qtd", seq * 16 + qt)], writes=[(qb, 0)], dma=True)

            def xload(qt, seq=seq):
                rs = f"res{qt % 2}"
                kb.op("sync", lambda e, seq=seq, qt=qt, rs=rs: e.dma_start(
                    out=t(rs)[:], in_=dram["x"][seq, qt * 128:(qt + 1) * 128, :]),
                    writes=[(rs, c) for c in range(4)], dma=True)

            def softmax_g(qt, kvh):
                qb = f"qt{qt % 2}"
                span, nch, widths, off = score_geom(qt)
                c_last = nch - 1
                par = (4 * qt + kvh) % 3
                G = 4 if nch == 1 else (2 if nch == 2 else 1)

                def geom(h4):
                    base = h4 if nch == 1 else ((h4 % 2) * 2 if nch == 2 else 0)
                    return base, [("ps", base + c) for c in range(nch)]

                for g0 in range(0, 4, G):
                    hs = list(range(g0, g0 + G))
                    for h4 in hs:
                        h = kvh * 4 + h4
                        base, bank_keys = geom(h4)
                        for c in range(nch):
                            last = (c == nch - 1)
                            kb.op("pe", lambda e, qb=qb, h=h, kvh=kvh, c=c,
                                  w=widths[c], base=base, last=last:
                                  e.matmul(psv(base + c, 0, w),
                                           t(qb)[:, h * 128:(h + 1) * 128],
                                           t("kT")[:, kvh * T + c * 512: kvh * T + c * 512 + w],
                                           start=True, stop=not last),
                                  reads=[(qb, 0)] + [("kTw", i) for i in
                                         range((c * 512) // 128,
                                               min(NTC, (c * 512 + widths[c] + 127) // 128))],
                                  writes=[bank_keys[c]])
                        kb.op("pe", lambda e, base=base, c=c_last, off=widths[c_last] - 128:
                              e.matmul(psv(base + c, off, off + 128),
                                       t("maskb")[:], t("ident")[:],
                                       start=False, stop=True),
                              reads=[("maskb", 0), ("ident", 0)],
                              writes=[bank_keys[-1]])
                        yield
                    for h4 in hs:
                        base, bank_keys = geom(h4)
                        cm = f"cmx{par}_{h4}"
                        for c in range(nch):
                            kb.op("dve", lambda e, base=base, c=c, w=widths[c], cm=cm:
                                  e.reduce_max(t(cm)[:, c:c + 1], psv(base + c, 0, w),
                                               axis=AX.X),
                                  reads=[bank_keys[c]], writes=[(cm, c)])
                    yield
                    for h4 in hs:
                        cm = f"cmx{par}_{h4}"
                        kb.op("dve", lambda e, cm=cm, nch=nch, h4=h4, par=par:
                              e.reduce_max(t(f"nm{par}_{h4}")[:], t(cm)[:, 0:nch],
                                           axis=AX.X, negate=True),
                              reads=[(cm, c) for c in range(nch)],
                              writes=[(f"nm{par}_{h4}", 0)])
                    yield
                    for h4 in hs:
                        base, bank_keys = geom(h4)
                        pb = f"pbf{par}_{h4}"
                        zp = f"zp{par}_{h4}"
                        for c in range(nch):
                            kb.op("act", lambda e, base=base, c=c, w=widths[c],
                                  pb=pb, zp=zp, h4=h4, par=par:
                                  e.activation(
                                      t(pb)[:, c * 512: c * 512 + w],
                                      psv(base + c, 0, w), AF.Exp,
                                      bias=t(f"nm{par}_{h4}")[:], scale=1.0,
                                      accum_out=t(zp)[:, c:c + 1]),
                                  reads=[bank_keys[c], (f"nm{par}_{h4}", 0)],
                                  writes=[(pb, c), (zp, c)])
                        yield
                    for h4 in hs:
                        zp = f"zp{par}_{h4}"
                        kb.op("dve", lambda e, zp=zp, nch=nch, h4=h4, par=par:
                              e.reduce_sum(t(f"zz{par}_{h4}")[:], t(zp)[:, 0:nch],
                                           axis=AX.X),
                              reads=[(zp, c) for c in range(nch)],
                              writes=[(f"zz{par}_{h4}", 0)])
                    for h4 in hs:
                        kb.op("dve", lambda e, h4=h4, par=par: e.reciprocal(
                            t(f"rz{par}_{h4}")[:], t(f"zz{par}_{h4}")[:]),
                            reads=[(f"zz{par}_{h4}", 0)], writes=[(f"rz{par}_{h4}", 0)])
                    for h4 in hs:
                        kb.op("dve", lambda e, h4=h4, par=par: e.tensor_scalar_mul(
                            t(f"diag4_{par}")[:, h4 * 128:(h4 + 1) * 128],
                            t("ident")[:], t(f"rz{par}_{h4}")[:]),
                            reads=[("ident", 0), (f"rz{par}_{h4}", 0)],
                            writes=[(f"diag4_{par}", h4)])
                    yield

            def ptpv_g(qt, kvh):
                tlq = qt % 2
                par = (4 * qt + kvh) % 3
                ob = 6 + kvh % 2

                def trans_cast(sblk):
                    ptb = 4 + sblk % 2
                    pt = f"pt4_{sblk % 2}"
                    for h4 in range(4):
                        kb.op("pe", lambda e, h4=h4, sblk=sblk, ptb=ptb, par=par:
                              e.matmul(
                                  psv(ptb, h4 * 128, (h4 + 1) * 128),
                                  t(f"pbf{par}_{h4}")[:, sblk * 128:(sblk + 1) * 128],
                                  t(f"diag4_{par}")[:, h4 * 128:(h4 + 1) * 128],
                                  start=True, stop=True),
                              reads=[(f"pbf{par}_{h4}", (sblk * 128) // 512),
                                     (f"diag4_{par}", h4)],
                              writes=[("ps", ptb)])
                    if sblk % 2 == 0:
                        kb.op("dve", lambda e, ptb=ptb, pt=pt: e.tensor_copy(
                            t(pt)[:], psv(ptb)),
                            reads=[("ps", ptb)], writes=[(pt, 0)])
                    else:
                        kb.op("act", lambda e, ptb=ptb, pt=pt: e.activation(
                            t(pt)[:], psv(ptb), AF.Copy),
                            reads=[("ps", ptb)], writes=[(pt, 0)])

                def pv(sblk):
                    pt = f"pt4_{sblk % 2}"
                    kb.op("pe", lambda e, sblk=sblk, kvh=kvh, pt=pt, qt=qt, ob=ob:
                          e.matmul(
                              psv(ob),
                              t("vres")[:, sblk * KD + kvh * 128: sblk * KD + (kvh + 1) * 128],
                              t(pt)[:], start=(sblk == 0), stop=(sblk == qt)),
                          reads=[("vres", sblk), (pt, 0)], writes=[("ps", ob)])

                for sblk in range(qt + 1):
                    trans_cast(sblk)
                    if sblk >= 1:
                        pv(sblk - 1)
                    yield
                pv(qt)
                kb.op("act", lambda e, kvh=kvh, tlq=tlq, ob=ob: e.activation(
                    t("ytg")[:].rearrange("p (d x) -> p d x", x=256)[
                        :, kvh * 4:(kvh + 1) * 4, tlq * 128:(tlq + 1) * 128],
                    psv(ob).rearrange("p (d x) -> p d x", x=128),
                    AF.Copy),
                    reads=[("ps", ob)], writes=[("ytg", (tlq, kvh))])

            def ln_steps(insts):
                for c4 in range(4):
                    for rs_name, out_name, li in insts:
                        kb.op("dve", lambda e, c4=c4, rs_name=rs_name, li=li: e.bn_stats(
                            t(f"bn24_{li}")[:, c4 * 6:(c4 + 1) * 6],
                            t(rs_name)[:, c4 * 512:(c4 + 1) * 512]),
                            reads=[(rs_name, c4)],
                            writes=[(f"bn24_{li}", 0)])
                for rs_name, out_name, li in insts:
                    kb.op("dve", lambda e, li=li: e.reduce_sum(
                        t(f"summ_{li}")[:],
                        t(f"bn24_{li}")[:].rearrange("p (g s) -> p s g", s=3)[:, 1, :],
                        axis=AX.X),
                        reads=[(f"bn24_{li}", 0)], writes=[(f"summ_{li}", 0)])
                    kb.op("dve", lambda e, li=li: e.reduce_sum(
                        t(f"sM2_{li}")[:],
                        t(f"bn24_{li}")[:].rearrange("p (g s) -> p s g", s=3)[:, 2, :],
                        axis=AX.X),
                        reads=[(f"bn24_{li}", 0)], writes=[(f"sM2_{li}", 0)])
                for rs_name, out_name, li in insts:
                    kb.op("dve", lambda e, li=li: e.tensor_scalar_mul(
                        t(f"meant_{li}")[:], t(f"summ_{li}")[:], 0.125),
                        reads=[(f"summ_{li}", 0)], writes=[(f"meant_{li}", 0)])
                    kb.op("dve", lambda e, li=li: e.tensor_scalar(
                        t(f"dev_{li}")[:],
                        t(f"bn24_{li}")[:].rearrange("p (g s) -> p s g", s=3)[:, 1, :],
                        t(f"meant_{li}")[:], None, op0=ALU.subtract),
                        reads=[(f"bn24_{li}", 0), (f"meant_{li}", 0)],
                        writes=[(f"dev_{li}", 0)])
                for rs_name, out_name, li in insts:
                    kb.op("dve", lambda e, li=li: e.tensor_mul(
                        t(f"dev2_{li}")[:], t(f"dev_{li}")[:], t(f"dev_{li}")[:]),
                        reads=[(f"dev_{li}", 0)], writes=[(f"dev2_{li}", 0)])
                for rs_name, out_name, li in insts:
                    kb.op("dve", lambda e, li=li: e.reduce_sum(
                        t(f"sdev_{li}")[:], t(f"dev2_{li}")[:], axis=AX.X),
                        reads=[(f"dev2_{li}", 0)], writes=[(f"sdev_{li}", 0)])
                    kb.op("dve", lambda e, li=li: e.tensor_scalar(
                        t(f"sM2_{li}")[:], t(f"sM2_{li}")[:], 1.0 / D, EPS_LN,
                        op0=ALU.mult, op1=ALU.add),
                        reads=[(f"sM2_{li}", 0)], writes=[(f"sM2_{li}", 0)])
                for rs_name, out_name, li in insts:
                    kb.op("dve", lambda e, li=li: e.scalar_tensor_tensor(
                        t(f"vart_{li}")[:], t(f"sdev_{li}")[:], 0.125, t(f"sM2_{li}")[:],
                        op0=ALU.mult, op1=ALU.add),
                        reads=[(f"sdev_{li}", 0), (f"sM2_{li}", 0)],
                        writes=[(f"vart_{li}", 0)])
                for rs_name, out_name, li in insts:
                    kb.op("act", lambda e, li=li: e.activation(
                        t(f"lnv_{li}")[:], t(f"vart_{li}")[:], AF.Ln),
                        reads=[(f"vart_{li}", 0)], writes=[(f"lnv_{li}", 0)])
                for rs_name, out_name, li in insts:
                    kb.op("dve", lambda e, li=li: e.tensor_scalar_mul(
                        t(f"lnv_{li}")[:], t(f"lnv_{li}")[:], -0.5),
                        reads=[(f"lnv_{li}", 0)], writes=[(f"lnv_{li}", 0)])
                for rs_name, out_name, li in insts:
                    kb.op("act", lambda e, li=li: e.activation(
                        t(f"rstd1_{li}")[:], t(f"lnv_{li}")[:], AF.Exp),
                        reads=[(f"lnv_{li}", 0)], writes=[(f"rstd1_{li}", 0)])
                for rs_name, out_name, li in insts:
                    kb.op("dve", lambda e, li=li: e.tensor_scalar(
                        t(f"nmr_{li}")[:], t(f"meant_{li}")[:], t(f"rstd1_{li}")[:], -1.0,
                        op0=ALU.mult, op1=ALU.mult),
                        reads=[(f"meant_{li}", 0), (f"rstd1_{li}", 0)],
                        writes=[(f"nmr_{li}", 0)])
                for rs_name, out_name, li in insts:
                    kb.op("act", lambda e, rs_name=rs_name, out_name=out_name, li=li:
                          e.activation(
                              t(out_name)[:], t(rs_name)[:], AF.Identity,
                              bias=t(f"nmr_{li}")[:], scale=t(f"rstd1_{li}")[:]),
                          reads=[(rs_name, i) for i in range(4)] +
                                [(f"nmr_{li}", 0), (f"rstd1_{li}", 0)],
                          writes=[(out_name, 0)])

            def proj_ln_store_g(qt, seq=seq):
                tl = qt % 2
                rs = f"res{qt % 2}"
                x1 = f"x1bf{qt % 2}"
                xb = f"xbf{qt % 2}"
                li = qt % 2
                kb.op("act", lambda e, rs=rs, xb=xb: e.activation(
                    t(xb)[:], t(rs)[:], AF.Copy),
                    reads=[(rs, c) for c in range(4)], writes=[(xb, 0)])
                for q in range(4):
                    bank = 4 + q % 2
                    kb.op("pe", lambda e, q=q, bank=bank, xb=xb: e.matmul(
                        psv(bank), t("ident")[:],
                        t(xb)[:, q * 512:(q + 1) * 512],
                        start=True, stop=False),
                        reads=[("ident", 0), (xb, 0)], writes=[("ps", bank)])
                    for dc in range(16):
                        kb.op("pe", lambda e, dc=dc, q=q, tl=tl, bank=bank: e.matmul(
                            psv(bank),
                            t("ytg")[:, dc * 256 + tl * 128: dc * 256 + (tl + 1) * 128],
                            t("wp")[:, dc * D + q * 512: dc * D + (q + 1) * 512],
                            start=False, stop=(dc == 15)),
                            reads=[("ytg", (tl, k)) for k in range(4)] + [("wp", dc)],
                            writes=[("ps", bank)])
                    kb.op("act", lambda e, q=q, bank=bank, rs=rs: e.activation(
                        t(rs)[:, q * 512:(q + 1) * 512], psv(bank), AF.Copy),
                        reads=[("ps", bank)], writes=[(rs, q)])
                    yield
                ln_steps([(rs, x1, li)])
                kb.op("act", lambda e, seq=seq, qt=qt, x1=x1: e.dma_start(
                    out=dram["x1k"][seq, qt], in_=t(x1)[:]),
                    reads=[(x1, 0)], writes=[("x1k", seq * 16 + qt)], dma=True)

            def pull(Q):
                try:
                    next(Q[0][1])
                    return True
                except StopIteration:
                    Q.pop(0)
                    return False

            def interleave(sm_gen, Q, forbid=None):
                if forbid is not None:
                    while any(tag == forbid for tag, _ in Q):
                        pull(Q)
                for _ in sm_gen:
                    pulls = 2
                    while pulls > 0 and Q:
                        if pull(Q):
                            pulls -= 1

            def drain(Q, upto=None):
                while Q:
                    tag = Q[0][0]
                    if not pull(Q):
                        if upto is not None and tag == upto:
                            return

            qload(0)
            qload(1)
            xload(0)
            Q = []
            for qt in range(NTC):
                for kvh in range(4):
                    u = 4 * qt + kvh
                    if kvh == 1 and qt > 0:
                        Q.append(("proj", proj_ln_store_g(qt - 1)))
                    interleave(softmax_g(qt, kvh), Q, forbid=u % 3)
                    Q.append((u % 3, ptpv_g(qt, kvh)))
                    if kvh == 2:
                        if qt > 0:
                            drain(Q, upto="proj")
                        if qt + 1 < NTC:
                            xload(qt + 1)
                if qt + 2 < NTC:
                    qload(qt + 2)
            drain(Q)
            drain([("proj", proj_ln_store_g(NTC - 1))])
        kb.close_scope(s2)
        kb.close_scope(sc)

    kb.open_scope("s3")
    with kb.phase("s3"):
        kb.sbuf("x1tg0", [128, 4 * D], BF16, "s3")
        for i in range(4):
            kb.sbuf(f"w1h{i}", [128, 8 * 128], BF16, "s3")
            kb.sbuf(f"w3h{i}", [128, 16 * 128], BF16, "s3")
        for i in range(6):
            kb.sbuf(f"w2h{i}", [128, 16 * 128], BF16, "s3")
        for i in range(3):
            kb.sbuf(f"otbf{i}", [128, 512], BF16, "s3")
        kb.sbuf("x1ks0", [128, 4 * D], BF16, "s3")
        kb.sbuf("x1ks1", [128, 4 * D], BF16, "s3")
        kb.sbuf("h1", [128, 32 * 512], BF16, "s3")
        kb.sbuf("h2", [128, 32 * 512], BF16, "s3")
        for i in range(4):
            kb.sbuf(f"res2_{i}", [128, D], F32, "s3")

        def x1_load_transpose(gidx):
            seq, g = divmod(gidx, 4)
            xk = f"x1ks{gidx % 2}"
            xg = "x1tg0"
            for tcl in range(4):
                kb.op("sync", lambda e, seq=seq, g=g, tcl=tcl, xk=xk: e.dma_start(
                    out=t(xk)[:, tcl * D:(tcl + 1) * D],
                    in_=dram["x1k"][seq, g * 4 + tcl]),
                    reads=[("x1k", seq * 16 + g * 4 + tcl)], writes=[(xk, tcl)], dma=True)
            for tcl in range(4):
                for g2 in range(4):
                    bank = 6 + (tcl * 4 + g2) % 2
                    for j in range(4):
                        dc = g2 * 4 + j
                        kb.op("pe", lambda e, tcl=tcl, dc=dc, j=j, bank=bank, xk=xk: e.matmul(
                            psv(bank, j * 128, (j + 1) * 128),
                            t(xk)[:, tcl * D + dc * 128: tcl * D + (dc + 1) * 128],
                            t("ident")[:], start=True, stop=True),
                            reads=[(xk, tcl), ("ident", 0)], writes=[("ps", bank)])
                    kb.op("dve", lambda e, tcl=tcl, g2=g2, bank=bank, xg=xg: e.tensor_copy(
                        t(xg)[:, tcl * D + g2 * 512: tcl * D + (g2 + 1) * 512], psv(bank)),
                        reads=[("ps", bank)], writes=[(xg, tcl)])

        for gidx in range(8):
            seq, g = divmod(gidx, 4)
            xg = "x1tg0"
            xk = f"x1ks{gidx % 2}"
            if gidx == 0:
                x1_load_transpose(0)

            def xg_view(dc, xg=xg):
                return t(xg)[:].rearrange("p (c n) -> p c n", c=4)[:, :, dc * 128:(dc + 1) * 128]

            RING4 = ("sync", "pool")
            for hc in range(32):
                for hf in range(2):
                    idx = 2 * hc + hf
                    ws = f"w1h{idx % 4}"
                    kb.op(RING4[idx % 2], lambda e, hc=hc, hf=hf, ws=ws: e.dma_start(
                        out=t(ws)[:],
                        in_=dram["w1d"][hc][:, hf * 8:(hf + 1) * 8, :].rearrange(
                            "p c n -> p (c n)")),
                        reads=[("w1d", hc // 16)], writes=[(ws, 0)], dma=True)
                bank = hc % 4
                for dc in range(16):
                    ws = f"w1h{(2 * hc + dc // 8) % 4}"
                    kb.op("pe", lambda e, dc=dc, ws=ws, bank=bank, xg=xg: e.matmul(
                        psv(bank), t(ws)[:, (dc % 8) * 128:(dc % 8 + 1) * 128],
                        xg_view(dc, xg), start=(dc == 0), stop=(dc == 15)),
                        reads=[(ws, 0)] + [(xg, i) for i in range(4)],
                        writes=[("ps", bank)])
                kb.op("act", lambda e, hc=hc, bank=bank: e.activation(
                    t("h1")[:, hc * 512:(hc + 1) * 512], psv(bank), AF.Relu,
                    bias=t("b1c")[:, hc:hc + 1], scale=1.0),
                    reads=[("ps", bank), ("b1c", 0)], writes=[("h1", hc)])
            for jc in range(32):
                for hf in range(2):
                    idx = 2 * jc + hf
                    ws = f"w2h{idx % 6}"
                    kb.op(RING4[idx % 2], lambda e, jc=jc, hf=hf, ws=ws: e.dma_start(
                        out=t(ws)[:],
                        in_=dram["w2d"][jc][:, hf * 16:(hf + 1) * 16, :].rearrange(
                            "p c n -> p (c n)")),
                        reads=[("w2d", jc // 8)], writes=[(ws, 0)], dma=True)
                bank = jc % 4
                for hc in range(32):
                    ws = f"w2h{(2 * jc + hc // 16) % 6}"
                    kb.op("pe", lambda e, hc=hc, ws=ws, bank=bank: e.matmul(
                        psv(bank), t(ws)[:, (hc % 16) * 128:(hc % 16 + 1) * 128],
                        t("h1")[:, hc * 512:(hc + 1) * 512],
                        start=(hc == 0), stop=(hc == 31)),
                        reads=[(ws, 0), ("h1", hc)], writes=[("ps", bank)])
                kb.op("act", lambda e, jc=jc, bank=bank: e.activation(
                    t("h2")[:, jc * 512:(jc + 1) * 512], psv(bank), AF.Relu,
                    bias=t("b2c")[:, jc:jc + 1], scale=1.0),
                    reads=[("ps", bank), ("b2c", 0)], writes=[("h2", jc)])
            if gidx + 1 < 8:
                x1_load_transpose(gidx + 1)
            for oc in range(16):
                ob = f"otbf{oc % 2}"
                for hf in range(2):
                    idx = 2 * oc + hf
                    ws = f"w3h{idx % 4}"
                    kb.op(RING4[idx % 2], lambda e, oc=oc, hf=hf, ws=ws: e.dma_start(
                        out=t(ws)[:],
                        in_=dram["w3d"][oc][:, hf * 16:(hf + 1) * 16, :].rearrange(
                            "p c n -> p (c n)")),
                        reads=[("w3d", oc // 8)], writes=[(ws, 0)], dma=True)
                bank = oc % 2
                for hc in range(32):
                    ws = f"w3h{(2 * oc + hc // 16) % 4}"
                    kb.op("pe", lambda e, hc=hc, ws=ws, bank=bank: e.matmul(
                        psv(bank), t(ws)[:, (hc % 16) * 128:(hc % 16 + 1) * 128],
                        t("h2")[:, hc * 512:(hc + 1) * 512],
                        start=(hc == 0), stop=(hc == 31)),
                        reads=[(ws, 0), ("h2", hc)], writes=[("ps", bank)])
                kb.op("act", lambda e, oc=oc, bank=bank, ob=ob: e.activation(
                    t(ob)[:], psv(bank), AF.Identity,
                    bias=t("b3c")[:, oc:oc + 1], scale=1.0),
                    reads=[("ps", bank), ("b3c", 0)], writes=[(ob, 0)])
                tbank = 4 + oc % 2
                for tl in range(4):
                    kb.op("pe", lambda e, tl=tl, ob=ob, tbank=tbank: e.matmul(
                        psv(tbank, tl * 128, (tl + 1) * 128),
                        t(ob)[:, tl * 128:(tl + 1) * 128], t("ident")[:],
                        start=True, stop=True),
                        reads=[(ob, 0), ("ident", 0)], writes=[("ps", tbank)])
                for tl in range(4):
                    kb.op("dve", lambda e, tl=tl, oc=oc, tbank=tbank, xk=xk: e.tensor_add(
                        t(f"res2_{tl}")[:, oc * 128:(oc + 1) * 128],
                        psv(tbank, tl * 128, (tl + 1) * 128),
                        t(xk)[:, tl * D + oc * 128: tl * D + (oc + 1) * 128]),
                        reads=[("ps", tbank), (xk, tl)],
                        writes=[(f"res2_{tl}", oc), (f"res2_{tl}", "ln")])
            insts = [(f"res2_{tl}", None, tl) for tl in range(4)]
            for c4 in range(4):
                for rs_name, out_name, li in insts:
                    kb.op("dve", lambda e, c4=c4, rs_name=rs_name, li=li: e.bn_stats(
                        t(f"bn24_{li}")[:, c4 * 6:(c4 + 1) * 6],
                        t(rs_name)[:, c4 * 512:(c4 + 1) * 512]),
                        reads=[(rs_name, i) for i in range(16)], writes=[(f"bn24_{li}", 0)])
            for rs_name, out_name, li in insts:
                kb.op("dve", lambda e, li=li: e.reduce_sum(
                    t(f"summ_{li}")[:],
                    t(f"bn24_{li}")[:].rearrange("p (g s) -> p s g", s=3)[:, 1, :],
                    axis=AX.X),
                    reads=[(f"bn24_{li}", 0)], writes=[(f"summ_{li}", 0)])
                kb.op("dve", lambda e, li=li: e.reduce_sum(
                    t(f"sM2_{li}")[:],
                    t(f"bn24_{li}")[:].rearrange("p (g s) -> p s g", s=3)[:, 2, :],
                    axis=AX.X),
                    reads=[(f"bn24_{li}", 0)], writes=[(f"sM2_{li}", 0)])
            for rs_name, out_name, li in insts:
                kb.op("dve", lambda e, li=li: e.tensor_scalar_mul(
                    t(f"meant_{li}")[:], t(f"summ_{li}")[:], 0.125),
                    reads=[(f"summ_{li}", 0)], writes=[(f"meant_{li}", 0)])
                kb.op("dve", lambda e, li=li: e.tensor_scalar(
                    t(f"dev_{li}")[:],
                    t(f"bn24_{li}")[:].rearrange("p (g s) -> p s g", s=3)[:, 1, :],
                    t(f"meant_{li}")[:], None, op0=ALU.subtract),
                    reads=[(f"bn24_{li}", 0), (f"meant_{li}", 0)], writes=[(f"dev_{li}", 0)])
            for rs_name, out_name, li in insts:
                kb.op("dve", lambda e, li=li: e.tensor_mul(
                    t(f"dev2_{li}")[:], t(f"dev_{li}")[:], t(f"dev_{li}")[:]),
                    reads=[(f"dev_{li}", 0)], writes=[(f"dev2_{li}", 0)])
                kb.op("dve", lambda e, li=li: e.reduce_sum(
                    t(f"sdev_{li}")[:], t(f"dev2_{li}")[:], axis=AX.X),
                    reads=[(f"dev2_{li}", 0)], writes=[(f"sdev_{li}", 0)])
                kb.op("dve", lambda e, li=li: e.tensor_scalar(
                    t(f"sM2_{li}")[:], t(f"sM2_{li}")[:], 1.0 / D, EPS_LN,
                    op0=ALU.mult, op1=ALU.add),
                    reads=[(f"sM2_{li}", 0)], writes=[(f"sM2_{li}", 0)])
            for rs_name, out_name, li in insts:
                kb.op("dve", lambda e, li=li: e.scalar_tensor_tensor(
                    t(f"vart_{li}")[:], t(f"sdev_{li}")[:], 0.125, t(f"sM2_{li}")[:],
                    op0=ALU.mult, op1=ALU.add),
                    reads=[(f"sdev_{li}", 0), (f"sM2_{li}", 0)], writes=[(f"vart_{li}", 0)])
            for rs_name, out_name, li in insts:
                kb.op("act", lambda e, li=li: e.activation(
                    t(f"lnv_{li}")[:], t(f"vart_{li}")[:], AF.Ln),
                    reads=[(f"vart_{li}", 0)], writes=[(f"lnv_{li}", 0)])
            for rs_name, out_name, li in insts:
                kb.op("dve", lambda e, li=li: e.tensor_scalar_mul(
                    t(f"lnv_{li}")[:], t(f"lnv_{li}")[:], -0.5),
                    reads=[(f"lnv_{li}", 0)], writes=[(f"lnv_{li}", 0)])
            for rs_name, out_name, li in insts:
                kb.op("act", lambda e, li=li: e.activation(
                    t(f"rstd1_{li}")[:], t(f"lnv_{li}")[:], AF.Exp),
                    reads=[(f"lnv_{li}", 0)], writes=[(f"rstd1_{li}", 0)])
            for rs_name, out_name, li in insts:
                kb.op("dve", lambda e, li=li: e.tensor_scalar(
                    t(f"nmr_{li}")[:], t(f"meant_{li}")[:], t(f"rstd1_{li}")[:], -1.0,
                    op0=ALU.mult, op1=ALU.mult),
                    reads=[(f"meant_{li}", 0), (f"rstd1_{li}", 0)], writes=[(f"nmr_{li}", 0)])
            for rs_name, out_name, li in insts:
                tl = li
                kb.op("act", lambda e, rs_name=rs_name, li=li: e.activation(
                    t(rs_name)[:], t(rs_name)[:], AF.Identity,
                    bias=t(f"nmr_{li}")[:], scale=t(f"rstd1_{li}")[:]),
                    reads=[(rs_name, i) for i in range(16)] +
                          [(f"nmr_{li}", 0), (f"rstd1_{li}", 0)],
                    writes=[(rs_name, "ln")])
                kb.op("act", lambda e, seq=seq, g=g, tl=tl, rs_name=rs_name: e.dma_start(
                    out=dram["y"][seq, (g * 4 + tl) * 128:(g * 4 + tl + 1) * 128, :],
                    in_=t(rs_name)[:]),
                    reads=[(rs_name, "ln")], writes=[("ydone", gidx * 4 + tl)], dma=True)
    kb.close_scope("s3")

    with kb.phase("fin"):
        def fin(e):
            tot = kb.totals()
            for semname, val in tot.items():
                if val > 0 and not semname.startswith("qsync"):
                    e.wait_ge(kb.sems[semname], val)
            for semname, val in tot.items():
                if val > 0 and semname.startswith("qsync"):
                    e.wait_ge(kb.sems[semname], val)
            return None
        kb.op("sync", fin)
    kb.close_scope("glob")


def _host_prep(inputs):
    f32 = np.float32
    x = np.ascontiguousarray(inputs["L0_attn_x"], dtype=f32)
    wq = np.asarray(inputs["L0_attn_wq"], dtype=f32)
    wk = np.asarray(inputs["L0_attn_wk"], dtype=f32)
    wv = np.asarray(inputs["L0_attn_wv"], dtype=f32)
    wp = np.asarray(inputs["L0_attn_wp"], dtype=f32)
    qg = np.asarray(inputs["L0_attn_qg"], dtype=f32)
    W1 = np.asarray(inputs["L0_ffn_W1"], dtype=f32)
    b1 = np.asarray(inputs["L0_ffn_b1"], dtype=f32)
    W2 = np.asarray(inputs["L0_ffn_W2"], dtype=f32)
    b2 = np.asarray(inputs["L0_ffn_b2"], dtype=f32)
    W3 = np.asarray(inputs["L0_ffn_W3"], dtype=f32)
    b3 = np.asarray(inputs["L0_ffn_b3"], dtype=f32)

    shared = {}
    shared["wqt"] = np.ascontiguousarray(wq.T)
    shared["wkt"] = np.ascontiguousarray(wk.T)
    shared["wvt"] = np.ascontiguousarray(wv.T)
    shared["wpt"] = np.ascontiguousarray(wp.T)
    W1T = W1.T
    W2T = W2.T
    W3T = W3.T
    shared["w1b"] = np.ascontiguousarray(
        W1T.reshape(16, 128, 32, 128).transpose(2, 1, 0, 3))
    shared["w2b"] = np.ascontiguousarray(
        W2T.reshape(32, 128, 32, 128).transpose(2, 1, 0, 3))
    shared["w3b"] = np.ascontiguousarray(
        W3T.reshape(32, 128, 16, 128).transpose(2, 1, 0, 3))
    shared["b1cd"] = np.ascontiguousarray(b1.reshape(32, 128).T)
    shared["b2cd"] = np.ascontiguousarray(b2.reshape(32, 128).T)
    shared["b3cd"] = np.ascontiguousarray(b3.reshape(16, 128).T)
    shared["qgi"] = np.ascontiguousarray(qg.reshape(1, 16))
    shared["identi"] = np.eye(128, dtype=ml_dtypes.bfloat16)
    shared["maski"] = np.triu(np.full((128, 128), NEG_BIG, dtype=f32), k=1)
    shared["masktb"] = np.ascontiguousarray(
        np.tril(np.full((128, 128), NEG_BIG), k=-1)).astype(ml_dtypes.bfloat16)
    inv = (1.0 / 10000.0 ** (np.arange(0, PD, 2, dtype=f32) / PD)).astype(f32)
    f = np.arange(T, dtype=f32)[:, None] * inv[None, :]
    cosf = np.cos(f).astype(f32)
    sinf = np.sin(f).astype(f32)
    cost = np.tile(cosf, (1, 16))
    sint = np.tile(sinf, (1, 16))
    shared["cosbd"] = np.ascontiguousarray(
        cost.reshape(16, 128, 128).transpose(1, 0, 2).reshape(128, T))
    shared["sinbd"] = np.ascontiguousarray(
        sint.reshape(16, 128, 128).transpose(1, 0, 2).reshape(128, T))

    in_maps = []
    for c in range(NCORES):
        xs = x[2 * c:2 * c + 2]
        m = dict(shared)
        m["x"] = xs
        m["xtb"] = np.ascontiguousarray(
            xs.reshape(2, 16, 128, 16, 128).transpose(0, 1, 4, 3, 2).reshape(
                2, 16, 128, 2048))
        in_maps.append(m)
    return in_maps


def _build_program():
    nc = bass.Bass("TRN2", target_bir_lowering=False, debug=False)
    dram = {}
    dram["x"] = nc.dram_tensor("x", [2, T, D], F32, kind="ExternalInput").ap()
    dram["xtb"] = nc.dram_tensor("xtb", [2, NTC, 128, D], F32, kind="ExternalInput").ap()
    dram["wqt"] = nc.dram_tensor("wqt", [D, D], F32, kind="ExternalInput").ap()
    dram["wkt"] = nc.dram_tensor("wkt", [D, KD], F32, kind="ExternalInput").ap()
    dram["wvt"] = nc.dram_tensor("wvt", [D, KD], F32, kind="ExternalInput").ap()
    dram["wpt"] = nc.dram_tensor("wpt", [D, D], F32, kind="ExternalInput").ap()
    dram["w1b"] = nc.dram_tensor("w1b", [32, 128, 16, 128], F32, kind="ExternalInput").ap()
    dram["w2b"] = nc.dram_tensor("w2b", [32, 128, 32, 128], F32, kind="ExternalInput").ap()
    dram["w3b"] = nc.dram_tensor("w3b", [16, 128, 32, 128], F32, kind="ExternalInput").ap()
    dram["b1c"] = nc.dram_tensor("b1cd", [128, 32], F32, kind="ExternalInput").ap()
    dram["b2c"] = nc.dram_tensor("b2cd", [128, 32], F32, kind="ExternalInput").ap()
    dram["b3c"] = nc.dram_tensor("b3cd", [128, 16], F32, kind="ExternalInput").ap()
    dram["qgi"] = nc.dram_tensor("qgi", [1, 16], F32, kind="ExternalInput").ap()
    dram["identi"] = nc.dram_tensor("identi", [128, 128], BF16, kind="ExternalInput").ap()
    dram["maski"] = nc.dram_tensor("maski", [128, 128], F32, kind="ExternalInput").ap()
    dram["masktb"] = nc.dram_tensor("masktb", [128, 128], BF16, kind="ExternalInput").ap()
    dram["cosb"] = nc.dram_tensor("cosbd", [128, T], F32, kind="ExternalInput").ap()
    dram["sinb"] = nc.dram_tensor("sinbd", [128, T], F32, kind="ExternalInput").ap()
    dram["qtd"] = nc.dram_tensor("qtd", [2, NTC, 128, D], BF16, kind="Internal").ap()
    dram["x1t"] = nc.dram_tensor("x1td", [2, NTC, 128, D], BF16, kind="Internal").ap()
    dram["x1k"] = nc.dram_tensor("x1kd", [2, NTC, 128, D], BF16, kind="Internal").ap()
    dram["wpd"] = nc.dram_tensor("wpd", [D, D], BF16, kind="Internal").ap()
    dram["w1d"] = nc.dram_tensor("w1d", [32, 128, 16, 128], BF16, kind="Internal").ap()
    dram["w2d"] = nc.dram_tensor("w2d", [32, 128, 32, 128], BF16, kind="Internal").ap()
    dram["w3d"] = nc.dram_tensor("w3d", [16, 128, 32, 128], BF16, kind="Internal").ap()
    dram["y"] = nc.dram_tensor("y", [2, T, D], F32, kind="ExternalOutput").ap()

    kb = KB(nc)
    build_all(kb, dram)
    kb.finish_plan()
    with ExitStack() as st:
        kb.make_sems(st)
        build_all(kb, dram)
    for s in KB.STREAMS:
        assert kb.pos[s] == len(kb.ops[s]), (s, kb.pos[s], len(kb.ops[s]))
    return nc


def kernel(**inputs):
    in_maps = _host_prep(inputs)
    nc = _build_program()
    res = run_bass_kernel_spmd(nc, in_maps, core_ids=list(range(NCORES)))
    outs = [np.asarray(r["y"], dtype=np.float32) for r in res.results]
    return np.concatenate(outs, axis=0)


if __name__ == "__main__":
    nc = bass.Bass("TRN2", target_bir_lowering=False)
    dram = {k: None for k in ()}
    kb = KB(nc)

    class _D(dict):
        def __getitem__(self, k):
            return FakeAP()

    class FakeAP:
        shape = (32, 128, 16, 128)

        def __getitem__(self, *_):
            return self

        def rearrange(self, *a, **k):
            return self

    build_all(kb, _D())
    for s in KB.STREAMS:
        nd = sum(1 for r in kb.ops[s] if r.dma)
        nn = sum(1 for r in kb.ops[s] if r.needed)
        print(f"{s}: ops={len(kb.ops[s])} dma={nd} milestones={nn}")
    print("sbuf bytes/partition by scope:", {k: v for k, v in kb.sbuf_bytes.items()})


# version 47
# speedup vs baseline: 1.1378x; 1.1378x over previous
import numpy as np
import ml_dtypes
from contextlib import ExitStack, contextmanager

import concourse.bass as bass
import concourse.mybir as mybir
from concourse.bass_utils import run_bass_kernel_spmd

F32 = mybir.dt.float32
BF16 = mybir.dt.bfloat16
AF = mybir.ActivationFunctionType
ALU = mybir.AluOpType
AX = mybir.AxisListType

NCORES = 8
T = 2048
D = 2048
NTC = 16
NH, NKV, HD, PD = 16, 4, 128, 16
KD = NKV * HD
H1 = 4096
EPS_RMS = float(np.finfo(np.float32).eps)
EPS_LN = 1e-5
RSQRT_HD = 1.0 / float(np.sqrt(HD))
NEG_BIG = -1.0e30


class _Op:
    __slots__ = ("emit", "dma", "deps", "needed", "ms", "dma_ord")

    def __init__(self, dma, deps):
        self.emit = None
        self.dma = dma
        self.deps = deps
        self.needed = False
        self.ms = 0
        self.dma_ord = -1


class KB:
    STREAMS = ("pe", "act", "dve", "pool", "sync")
    ENG_OF = dict(pe="tensor", act="scalar", dve="vector", pool="gpsimd", sync="sync")
    NR = 16

    def __init__(self, nc):
        self.nc = nc
        self.mode = "plan"
        self.ops = {s: [] for s in self.STREAMS}
        self.pos = {s: 0 for s in self.STREAMS}
        self.dma_count = {s: 0 for s in self.STREAMS}
        self.buf = {}
        self.T = {}
        self.scope_stacks = {}
        self.scope_tiles = {}
        self.fence_due = {s: [] for s in self.STREAMS}
        self.highwater = {s: {} for s in self.STREAMS}
        self.sems = {}
        self.sbuf_bytes = {}

    def sbuf(self, name, shape, dtype, scope):
        if self.mode == "plan":
            esz = 2 if dtype == BF16 else 4
            self.sbuf_bytes[scope] = self.sbuf_bytes.get(scope, 0) + shape[-1] * esz
            self.scope_tiles.setdefault(scope, []).append(name)
            return name
        t = self.scope_stacks[scope].enter_context(
            self.nc.sbuf_tensor(f"{name}_{scope}", list(shape), dtype))
        self.T[name] = t
        return name

    def t(self, name):
        return self.T[name]

    def open_scope(self, name):
        if self.mode == "plan":
            self.scope_tiles.setdefault(name, [])
        else:
            self.scope_stacks[name] = ExitStack()

    def close_scope(self, name):
        if self.mode == "plan":
            handles = set()
            names = set(self.scope_tiles.get(name, ()))
            for key, ent in self.buf.items():
                if key[0] in names:
                    if ent[0] is not None:
                        handles.add(ent[0])
                    handles.update(ent[1])
            for s in self.STREAMS:
                self.fence_due[s].extend(handles)
        else:
            self.scope_stacks[name].close()
            del self.scope_stacks[name]

    def op(self, stream, emit, reads=(), writes=(), dma=False):
        if self.mode == "plan":
            handle = (stream, len(self.ops[stream]))
            deps = {}

            def add(h):
                if h is None:
                    return
                s, i = h
                if s == stream:
                    if s == "pe":
                        return
                    if dma and self.ops[s][i].dma:
                        return
                    if (not dma and not self.ops[s][i].dma
                            and len(self.ops[s]) - i > 4):
                        return
                deps[(s, i)] = True

            for h in self.fence_due[stream]:
                add(h)
            self.fence_due[stream] = []
            for k in reads:
                ent = self.buf.get(k)
                if ent is not None and ent[0] is not None:
                    add(ent[0])
            for k in writes:
                ent = self.buf.get(k)
                if ent is not None:
                    if ent[0] is not None:
                        add(ent[0])
                    for r in ent[1]:
                        add(r)
            best = {}
            for (s, i) in deps:
                o = self.ops[s][i]
                kd = (s, "d", (o.dma_ord - 1) % self.NR) if o.dma else (s, "c")
                if kd not in best or best[kd] < i:
                    best[kd] = i
            dl = sorted({(k[0], v) for k, v in best.items()})
            rec = _Op(dma, dl)
            if dma:
                self.dma_count[stream] += 1
                rec.dma_ord = self.dma_count[stream]
            self.ops[stream].append(rec)
            for (s, i) in dl:
                tgt = self.ops[s][i]
                if not tgt.dma:
                    tgt.needed = True
            for k in reads:
                self.buf.setdefault(k, [None, []])[1].append(handle)
            for k in writes:
                self.buf[k] = [handle, []]
            return handle
        else:
            i = self.pos[stream]
            self.pos[stream] += 1
            rec = self.ops[stream][i]
            assert rec.dma == dma, f"plan/emit divergence on {stream}[{i}]"
            rec.emit = emit
            return (stream, i)

    def finish_plan(self):
        for s in self.STREAMS:
            ms = 0
            ordn = 0
            for rec in self.ops[s]:
                if rec.dma:
                    ordn += 1
                    rec.dma_ord = ordn
                elif rec.needed:
                    ms += 1
                    rec.ms = ms
        self.buf = {}
        self.fence_due = {s: [] for s in self.STREAMS}
        self.mode = "emit"

    def make_sems(self, stack):
        for s in self.STREAMS:
            self.sems["s" + s] = stack.enter_context(self.nc.semaphore("s" + s))
            for r in range(self.NR):
                self.sems[f"q{s}_{r}"] = stack.enter_context(
                    self.nc.semaphore(f"q{s}_{r}"))

    def waitval(self, handle):
        s, i = handle
        rec = self.ops[s][i]
        if rec.dma:
            o = rec.dma_ord - 1
            return (f"q{s}_{o % self.NR}", 16 * (o // self.NR + 1))
        return ("s" + s, rec.ms)

    def totals(self):
        out = {}
        for s in self.STREAMS:
            nd = sum(1 for r in self.ops[s] if r.dma)
            nm = max([r.ms for r in self.ops[s] if not r.dma], default=0)
            for r in range(self.NR):
                cnt = (nd - r + self.NR - 1) // self.NR if nd > r else 0
                out[f"q{s}_{r}"] = 16 * cnt
            out["s" + s] = nm
        return out

    @contextmanager
    def phase(self, name):
        if self.mode == "plan":
            yield
            return
        start = {s: self.pos[s] for s in self.STREAMS}
        yield
        end = {s: self.pos[s] for s in self.STREAMS}
        with self.nc.Block(name) as blk:
            for s in self.STREAMS:
                if end[s] == start[s]:
                    continue
                dec = getattr(blk, self.ENG_OF[s])

                def body(eng, s=s, a=start[s], b=end[s]):
                    hw = self.highwater[s]
                    for i in range(a, b):
                        rec = self.ops[s][i]
                        for dep in rec.deps:
                            sem, val = self.waitval(dep)
                            if hw.get(sem, 0) < val:
                                eng.wait_ge(self.sems[sem], val)
                                hw[sem] = val
                        inst = rec.emit(eng)
                        if inst is None:
                            continue
                        if rec.dma:
                            inst.then_inc(
                                self.sems[f"q{s}_{(rec.dma_ord - 1) % self.NR}"], 16)
                        elif rec.needed:
                            inst.then_inc(self.sems["s" + s], 1)

                dec(body)


def build_all(kb, dram):
    nc = kb.nc
    t = kb.t
    PS = [f"ps{i}" for i in range(8)]

    def psv(i, a=0, b=512):
        return t(PS[i])[:, a:b]

    def score_geom(qt):
        span = (qt + 1) * 128
        nch = (span + 511) // 512
        widths = [min(512, span - 512 * c) for c in range(nch)]
        off = widths[-1] - 128
        return span, nch, widths, off

    kb.open_scope("glob")
    with kb.phase("setup"):
        for i in range(8):
            if kb.mode == "plan":
                kb.scope_tiles.setdefault("glob", []).append(PS[i])
            else:
                kb.T[PS[i]] = kb.scope_stacks["glob"].enter_context(
                    nc.psum_tensor(PS[i], [128, 512], F32))
        kb.sbuf("ident", [128, 128], BF16, "glob")
        kb.sbuf("maskt", [128, 128], F32, "glob")
        kb.sbuf("maskb", [128, 128], BF16, "glob")
        kb.sbuf("b1c", [128, 32], F32, "glob")
        kb.sbuf("b2c", [128, 32], F32, "glob")
        kb.sbuf("b3c", [128, 16], F32, "glob")
        kb.sbuf("qgsb", [1, 16], F32, "glob")
        kb.sbuf("ones1", [1, 128], F32, "glob")
        names = [("qgs", 16)]
        for p in range(3):
            for h4 in range(4):
                names += [(f"nm{p}_{h4}", 1), (f"zz{p}_{h4}", 1), (f"rz{p}_{h4}", 1),
                          (f"cmx{p}_{h4}", 4), (f"zp{p}_{h4}", 4)]
        for i in range(4):
            names += [(f"bn24_{i}", 24), (f"summ_{i}", 1), (f"sM2_{i}", 1),
                      (f"meant_{i}", 1), (f"dev_{i}", 8), (f"dev2_{i}", 8),
                      (f"sdev_{i}", 1), (f"vart_{i}", 1), (f"lnv_{i}", 1),
                      (f"rstd1_{i}", 1), (f"nmr_{i}", 1)]
        for p in range(2):
            names += [(f"ssq{p}", 16), (f"ssk{p}", 4), (f"rqraw{p}", 16),
                      (f"rkraw{p}", 4), (f"rstdq{p}", 16), (f"lnt{p}", 20)]
        for nm, w in names:
            kb.sbuf(nm, [128, w], F32, "glob")

        kb.op("sync", lambda e: e.dma_start(out=t("ident")[:], in_=dram["identi"]),
              writes=[("ident", 0)], dma=True)
        kb.op("sync", lambda e: e.dma_start(out=t("maskt")[:], in_=dram["maski"]),
              writes=[("maskt", 0)], dma=True)
        kb.op("sync", lambda e: e.dma_start(out=t("maskb")[:], in_=dram["masktb"]),
              writes=[("maskb", 0)], dma=True)
        kb.op("sync", lambda e: e.dma_start(out=t("b1c")[:], in_=dram["b1c"]),
              writes=[("b1c", 0)], dma=True)
        kb.op("sync", lambda e: e.dma_start(out=t("b2c")[:], in_=dram["b2c"]),
              writes=[("b2c", 0)], dma=True)
        kb.op("sync", lambda e: e.dma_start(out=t("b3c")[:], in_=dram["b3c"]),
              writes=[("b3c", 0)], dma=True)
        kb.op("sync", lambda e: e.dma_start(out=t("qgsb")[:], in_=dram["qgi"]),
              writes=[("qgsb", 0)], dma=True)
        kb.op("pool", lambda e: e.memset(t("ones1")[:], 1.0), writes=[("ones1", 0)])
        kb.op("pe", lambda e: e.matmul(psv(7, 0, 16), t("ones1")[:], t("qgsb")[:],
                                       start=True, stop=True),
              reads=[("ones1", 0), ("qgsb", 0)], writes=[("ps", 7)])
        kb.op("dve", lambda e: e.tensor_scalar_mul(t("qgs")[:], psv(7, 0, 16), RSQRT_HD),
              reads=[("ps", 7)], writes=[("qgs", 0)])
        def emit_w_casts():
            for nm, srcn, dstn, parts in (("w1d", "w1b", "w1d", 2), ("w2d", "w2b", "w2d", 4),
                                          ("w3d", "w3b", "w3d", 2)):
                n0 = dram[srcn].shape[0]
                step = n0 // parts
                for p in range(parts):
                    kb.op("pool",
                          lambda e, srcn=srcn, dstn=dstn, p=p, step=step:
                              e.dma_start(out=dram[dstn][p * step:(p + 1) * step],
                                          in_=dram[srcn][p * step:(p + 1) * step]),
                          writes=[(nm, p)], dma=True)
        kb.emit_w_casts = emit_w_casts

    for seq in range(2):
        sc = f"seq{seq}"
        kb.open_scope(sc)
        s1 = f"s1_{seq}"
        kb.open_scope(s1)
        with kb.phase(s1):
            kb.sbuf("kT", [128, NKV * T], BF16, sc)
            kb.sbuf("vres", [128, NTC * KD], BF16, sc)
            kb.sbuf("wq", [128, 16 * D], BF16, s1)
            kb.sbuf("wk", [128, 16 * KD], BF16, s1)
            kb.sbuf("wv", [128, 16 * KD], BF16, s1)
            kb.sbuf("cosb", [128, T], F32, s1)
            kb.sbuf("sinb", [128, T], F32, s1)
            for i in range(2):
                kb.sbuf(f"xt{i}", [128, D], BF16, s1)
                kb.sbuf(f"qtc{i}", [128, D], BF16, s1)
                kb.sbuf(f"qtok{i}", [128, D], F32, s1)
                kb.sbuf(f"qbf{i}", [128, D], BF16, s1)
            kb.sbuf("ktok0", [128, KD], F32, s1)
            kb.sbuf("kbf0", [128, KD], BF16, s1)
            kb.sbuf("kbf1", [128, KD], BF16, s1)
            kb.sbuf("scr0", [128, D], F32, s1)
            kb.sbuf("scrk0", [128, KD], F32, s1)
            kb.sbuf("tm1", [128, 128], F32, s1)
            kb.sbuf("tm2", [128, 128], F32, s1)
            kb.sbuf("tm3", [128, 128], F32, s1)
            kb.sbuf("tm4", [128, 128], F32, s1)

            def xload_s1(tc, seq=seq):
                xt = f"xt{tc % 2}"
                kb.op("pool", lambda e, seq=seq, tc=tc, xt=xt: e.dma_start(
                    out=t(xt)[:], in_=dram["xtb"][seq, tc]),
                    writes=[(xt, 0)], dma=True)

            xload_s1(0)
            xload_s1(1)
            for dc in range(16):
                kb.op("pool", lambda e, dc=dc: e.dma_start(
                    out=t("wq")[:, dc * D:(dc + 1) * D],
                    in_=dram["wqt"][dc * 128:(dc + 1) * 128, :]),
                    writes=[("wq", dc)], dma=True)
                kb.op("pool", lambda e, dc=dc: e.dma_start(
                    out=t("wk")[:, dc * KD:(dc + 1) * KD],
                    in_=dram["wkt"][dc * 128:(dc + 1) * 128, :]),
                    writes=[("wk", dc)], dma=True)
                kb.op("pool", lambda e, dc=dc: e.dma_start(
                    out=t("wv")[:, dc * KD:(dc + 1) * KD],
                    in_=dram["wvt"][dc * 128:(dc + 1) * 128, :]),
                    writes=[("wv", dc)], dma=True)
            kb.op("sync", lambda e: e.dma_start(out=t("cosb")[:], in_=dram["cosb"]),
                  writes=[("cosb", 0)], dma=True)
            kb.op("sync", lambda e: e.dma_start(out=t("sinb")[:], in_=dram["sinb"]),
                  writes=[("sinb", 0)], dma=True)

            def s1_front(tc, seq=seq):
                bi = tc % 2
                xt = f"xt{bi}"
                qtok = f"qtok{bi}"
                ktok = "ktok0"
                qbf = f"qbf{bi}"
                kbf = f"kbf{bi}"
                for dc in range(16):
                    for q in range(4):
                        kb.op("pe", lambda e, xt=xt, dc=dc, q=q: e.matmul(
                            psv(q), t(xt)[:, dc * 128:(dc + 1) * 128],
                            t("wq")[:, dc * D + q * 512: dc * D + (q + 1) * 512],
                            start=(dc == 0), stop=(dc == 15)),
                            reads=[(xt, 0), ("wq", dc)], writes=[("ps", q)])
                    kb.op("pe", lambda e, xt=xt, dc=dc: e.matmul(
                        psv(4), t(xt)[:, dc * 128:(dc + 1) * 128],
                        t("wk")[:, dc * KD:(dc + 1) * KD],
                        start=(dc == 0), stop=(dc == 15)),
                        reads=[(xt, 0), ("wk", dc)], writes=[("ps", 4)])
                    kb.op("pe", lambda e, xt=xt, dc=dc: e.matmul(
                        psv(5), t(xt)[:, dc * 128:(dc + 1) * 128],
                        t("wv")[:, dc * KD:(dc + 1) * KD],
                        start=(dc == 0), stop=(dc == 15)),
                        reads=[(xt, 0), ("wv", dc)], writes=[("ps", 5)])
                for q in range(4):
                    kb.op("act", lambda e, q=q, qtok=qtok: e.activation(
                        t(qtok)[:, q * 512:(q + 1) * 512], psv(q), AF.Copy),
                        reads=[("ps", q)], writes=[(qtok, 0)])
                kb.op("act", lambda e, ktok=ktok: e.activation(t(ktok)[:], psv(4), AF.Copy),
                      reads=[("ps", 4)], writes=[(ktok, 0)])
                kb.op("act", lambda e, tc=tc: e.activation(
                    t("vres")[:, tc * KD:(tc + 1) * KD], psv(5), AF.Copy),
                    reads=[("ps", 5)], writes=[("vres", tc)])
                scr, scrk = "scr0", "scrk0"
                ssq, ssk = f"ssq{bi}", f"ssk{bi}"
                lnt, rqraw, rkraw, rstdq = f"lnt{bi}", f"rqraw{bi}", f"rkraw{bi}", f"rstdq{bi}"
                kb.op("act", lambda e, qtok=qtok, scr=scr: e.activation(t(scr)[:], t(qtok)[:], AF.Square),
                      reads=[(qtok, 0)], writes=[(scr, 0)])
                kb.op("act", lambda e, ktok=ktok, scrk=scrk: e.activation(t(scrk)[:], t(ktok)[:], AF.Square),
                      reads=[(ktok, 0)], writes=[(scrk, 0)])
                kb.op("dve", lambda e, scr=scr, ssq=ssq: e.reduce_sum(
                    t(ssq)[:], t(scr)[:].rearrange("p (h j) -> p h j", j=128), axis=AX.X),
                    reads=[(scr, 0)], writes=[(ssq, 0)])
                kb.op("dve", lambda e, scrk=scrk, ssk=ssk: e.reduce_sum(
                    t(ssk)[:], t(scrk)[:].rearrange("p (h j) -> p h j", j=128), axis=AX.X),
                    reads=[(scrk, 0)], writes=[(ssk, 0)])
                kb.op("dve", lambda e, ssq=ssq: e.tensor_scalar(
                    t(ssq)[:], t(ssq)[:], 1.0 / HD, EPS_RMS, op0=ALU.mult, op1=ALU.add),
                    reads=[(ssq, 0)], writes=[(ssq, 0)])
                kb.op("dve", lambda e, ssk=ssk: e.tensor_scalar(
                    t(ssk)[:], t(ssk)[:], 1.0 / HD, EPS_RMS, op0=ALU.mult, op1=ALU.add),
                    reads=[(ssk, 0)], writes=[(ssk, 0)])
                kb.op("act", lambda e, ssq=ssq, lnt=lnt: e.activation(t(lnt)[:, 0:16], t(ssq)[:], AF.Ln),
                      reads=[(ssq, 0)], writes=[(lnt, 0)])
                kb.op("act", lambda e, ssk=ssk, lnt=lnt: e.activation(t(lnt)[:, 16:20], t(ssk)[:], AF.Ln),
                      reads=[(ssk, 0)], writes=[(lnt, 1)])
                kb.op("dve", lambda e, lnt=lnt: e.tensor_scalar_mul(t(lnt)[:], t(lnt)[:], -0.5),
                      reads=[(lnt, 0), (lnt, 1)], writes=[(lnt, 0), (lnt, 1)])
                kb.op("act", lambda e, lnt=lnt, rqraw=rqraw: e.activation(t(rqraw)[:], t(lnt)[:, 0:16], AF.Exp),
                      reads=[(lnt, 0)], writes=[(rqraw, 0)])
                kb.op("act", lambda e, lnt=lnt, rkraw=rkraw: e.activation(t(rkraw)[:], t(lnt)[:, 16:20], AF.Exp),
                      reads=[(lnt, 1)], writes=[(rkraw, 0)])
                kb.op("dve", lambda e, rqraw=rqraw, rstdq=rstdq: e.tensor_mul(
                    t(rstdq)[:], t(rqraw)[:], t("qgs")[:]),
                    reads=[(rqraw, 0), ("qgs", 0)], writes=[(rstdq, 0)])
                for h in range(16):
                    kb.op("dve", lambda e, h=h, qtok=qtok, rstdq=rstdq: e.tensor_scalar_mul(
                        t(qtok)[:, h * 128:(h + 1) * 128],
                        t(qtok)[:, h * 128:(h + 1) * 128], t(rstdq)[:, h:h + 1]),
                        reads=[(qtok, 0), (rstdq, 0)], writes=[(qtok, 0)])
                for h in range(4):
                    kb.op("dve", lambda e, h=h, ktok=ktok, rkraw=rkraw: e.tensor_scalar_mul(
                        t(ktok)[:, h * 128:(h + 1) * 128],
                        t(ktok)[:, h * 128:(h + 1) * 128], t(rkraw)[:, h:h + 1]),
                        reads=[(ktok, 0), (rkraw, 0)], writes=[(ktok, 0)])

                def rope(name, nh, tc=tc):
                    v1 = lambda: t(name)[:].rearrange("p (h d) -> p h d", d=128)[:, 0:nh, 0:8]
                    v2 = lambda: t(name)[:].rearrange("p (h d) -> p h d", d=128)[:, 0:nh, 8:16]
                    cv = lambda: t("cosb")[:, tc * 128:(tc + 1) * 128].rearrange(
                        "p (h j) -> p h j", j=8)[:, 0:nh, :]
                    sv = lambda: t("sinb")[:, tc * 128:(tc + 1) * 128].rearrange(
                        "p (h j) -> p h j", j=8)[:, 0:nh, :]
                    tv = lambda nm: t(nm)[:].rearrange("p (h j) -> p h j", j=8)[:, 0:nh, :]
                    key = (name, 0)
                    kb.op("dve", lambda e: e.tensor_mul(tv("tm1"), v1(), cv()),
                          reads=[key, ("cosb", 0)], writes=[("tm1", 0)])
                    kb.op("dve", lambda e: e.tensor_mul(tv("tm2"), v2(), sv()),
                          reads=[key, ("sinb", 0)], writes=[("tm2", 0)])
                    kb.op("dve", lambda e: e.tensor_mul(tv("tm3"), v1(), sv()),
                          reads=[key, ("sinb", 0)], writes=[("tm3", 0)])
                    kb.op("dve", lambda e: e.tensor_mul(tv("tm4"), v2(), cv()),
                          reads=[key, ("cosb", 0)], writes=[("tm4", 0)])
                    kb.op("dve", lambda e: e.tensor_add(v1(), tv("tm1"), tv("tm2")),
                          reads=[("tm1", 0), ("tm2", 0)], writes=[key])
                    kb.op("dve", lambda e: e.tensor_sub(v2(), tv("tm3"), tv("tm4")),
                          reads=[("tm3", 0), ("tm4", 0)], writes=[key])

                rope(qtok, 16)
                rope(ktok, 4)
                kb.op("dve", lambda e, qbf=qbf, qtok=qtok: e.tensor_copy(t(qbf)[:], t(qtok)[:]),
                      reads=[(qtok, 0)], writes=[(qbf, 0)])
                kb.op("dve", lambda e, kbf=kbf, ktok=ktok: e.tensor_copy(t(kbf)[:], t(ktok)[:]),
                      reads=[(ktok, 0)], writes=[(kbf, 0)])

            def s1_transposes(tc, seq=seq):
                bi = tc % 2
                qbf = f"qbf{bi}"
                kbf = f"kbf{bi}"
                qtc = f"qtc{bi}"
                for g in range(4):
                    bank = 6 + g % 2
                    for j in range(4):
                        hc = g * 4 + j
                        kb.op("pe", lambda e, hc=hc, bank=bank, j=j, qbf=qbf: e.matmul(
                            psv(bank, j * 128, (j + 1) * 128),
                            t(qbf)[:, hc * 128:(hc + 1) * 128], t("ident")[:],
                            start=True, stop=True),
                            reads=[(qbf, 0), ("ident", 0)], writes=[("ps", bank)])
                    kb.op("dve", lambda e, g=g, bank=bank, qtc=qtc: e.tensor_copy(
                        t(qtc)[:, g * 512:(g + 1) * 512], psv(bank)),
                        reads=[("ps", bank)], writes=[(qtc, 0)])
                for j in range(4):
                    kb.op("pe", lambda e, j=j, kbf=kbf: e.matmul(
                        psv(6, j * 128, (j + 1) * 128),
                        t(kbf)[:, j * 128:(j + 1) * 128], t("ident")[:],
                        start=True, stop=True),
                        reads=[(kbf, 0), ("ident", 0)], writes=[("ps", 6)])
                kb.op("dve", lambda e, tc=tc: e.tensor_copy(
                    t("kT")[:].rearrange("p (k n) -> p k n", k=4)[:, :, tc * 128:(tc + 1) * 128],
                    psv(6).rearrange("p (k n) -> p k n", k=4)),
                    reads=[("ps", 6)], writes=[("kTw", tc)])
                kb.op("act", lambda e, seq=seq, tc=tc, qtc=qtc: e.dma_start(
                    out=dram["qtd"][seq, tc], in_=t(qtc)[:]),
                    reads=[(qtc, 0)], writes=[("qtd", seq * 16 + tc)], dma=True)

            for tc in range(NTC):
                if 1 <= tc < NTC - 1:
                    xload_s1(tc + 1)
                s1_front(tc)
                if tc > 0:
                    s1_transposes(tc - 1)
            s1_transposes(NTC - 1)
        kb.close_scope(s1)

        s2 = f"s2_{seq}"
        kb.open_scope(s2)
        with kb.phase(s2):
            kb.sbuf("wp", [128, 16 * D], BF16, s2)
            for i in range(2):
                kb.sbuf(f"qt{i}", [128, D], BF16, s2)
                kb.sbuf(f"pt4_{i}", [128, 512], BF16, s2)
                kb.sbuf(f"res{i}", [128, D], F32, s2)
                kb.sbuf(f"x1bf{i}", [128, D], BF16, s2)
                kb.sbuf(f"xbf{i}", [128, D], BF16, s2)
            for i in range(3):
                kb.sbuf(f"diag4_{i}", [128, 512], BF16, s2)
                for h4 in range(4):
                    kb.sbuf(f"pbf{i}_{h4}", [128, T], BF16, s2)
            kb.sbuf("ytg", [128, 16 * 256], BF16, s2)

            for dc in range(16):
                kb.op("pool", lambda e, dc=dc: e.dma_start(
                    out=t("wp")[:, dc * D:(dc + 1) * D],
                    in_=dram["wpt"][dc * 128:(dc + 1) * 128, :]),
                    writes=[("wp", dc)], dma=True)
            if seq == 0:
                kb.emit_w_casts()

            def qload(qt, seq=seq):
                qb = f"qt{qt % 2}"
                kb.op("sync", lambda e, seq=seq, qt=qt, qb=qb: e.dma_start(
                    out=t(qb)[:], in_=dram["qtd"][seq, qt]),
                    reads=[("qtd", seq * 16 + qt)], writes=[(qb, 0)], dma=True)

            def xload(qt, seq=seq):
                rs = f"res{qt % 2}"
                kb.op("sync", lambda e, seq=seq, qt=qt, rs=rs: e.dma_start(
                    out=t(rs)[:], in_=dram["x"][seq, qt * 128:(qt + 1) * 128, :]),
                    writes=[(rs, c) for c in range(4)], dma=True)

            def softmax_g(qt, kvh):
                qb = f"qt{qt % 2}"
                span, nch, widths, off = score_geom(qt)
                c_last = nch - 1
                par = (4 * qt + kvh) % 3
                G = 4 if nch == 1 else (2 if nch == 2 else 1)

                def geom(h4):
                    base = h4 if nch == 1 else ((h4 % 2) * 2 if nch == 2 else 0)
                    return base, [("ps", base + c) for c in range(nch)]

                def ztail(hs):
                    for h4 in hs:
                        zp = f"zp{par}_{h4}"
                        kb.op("dve", lambda e, zp=zp, nch=nch, h4=h4, par=par:
                              e.reduce_sum(t(f"zz{par}_{h4}")[:], t(zp)[:, 0:nch],
                                           axis=AX.X),
                              reads=[(zp, c) for c in range(nch)],
                              writes=[(f"zz{par}_{h4}", 0)])
                    for h4 in hs:
                        kb.op("dve", lambda e, h4=h4, par=par: e.reciprocal(
                            t(f"rz{par}_{h4}")[:], t(f"zz{par}_{h4}")[:]),
                            reads=[(f"zz{par}_{h4}", 0)], writes=[(f"rz{par}_{h4}", 0)])
                    for h4 in hs:
                        kb.op("dve", lambda e, h4=h4, par=par: e.tensor_scalar_mul(
                            t(f"diag4_{par}")[:, h4 * 128:(h4 + 1) * 128],
                            t("ident")[:], t(f"rz{par}_{h4}")[:]),
                            reads=[("ident", 0), (f"rz{par}_{h4}", 0)],
                            writes=[(f"diag4_{par}", h4)])

                prev_hs = None
                for g0 in range(0, 4, G):
                    hs = list(range(g0, g0 + G))
                    for h4 in hs:
                        h = kvh * 4 + h4
                        base, bank_keys = geom(h4)
                        for c in range(nch):
                            last = (c == nch - 1)
                            kb.op("pe", lambda e, qb=qb, h=h, kvh=kvh, c=c,
                                  w=widths[c], base=base, last=last:
                                  e.matmul(psv(base + c, 0, w),
                                           t(qb)[:, h * 128:(h + 1) * 128],
                                           t("kT")[:, kvh * T + c * 512: kvh * T + c * 512 + w],
                                           start=True, stop=not last),
                                  reads=[(qb, 0)] + [("kTw", i) for i in
                                         range((c * 512) // 128,
                                               min(NTC, (c * 512 + widths[c] + 127) // 128))],
                                  writes=[bank_keys[c]])
                        kb.op("pe", lambda e, base=base, c=c_last, off=widths[c_last] - 128:
                              e.matmul(psv(base + c, off, off + 128),
                                       t("maskb")[:], t("ident")[:],
                                       start=False, stop=True),
                              reads=[("maskb", 0), ("ident", 0)],
                              writes=[bank_keys[-1]])
                        yield
                    for h4 in hs:
                        base, bank_keys = geom(h4)
                        cm = f"cmx{par}_{h4}"
                        for c in range(nch):
                            kb.op("dve", lambda e, base=base, c=c, w=widths[c], cm=cm:
                                  e.reduce_max(t(cm)[:, c:c + 1], psv(base + c, 0, w),
                                               axis=AX.X),
                                  reads=[bank_keys[c]], writes=[(cm, c)])
                    yield
                    for h4 in hs:
                        cm = f"cmx{par}_{h4}"
                        kb.op("dve", lambda e, cm=cm, nch=nch, h4=h4, par=par:
                              e.reduce_max(t(f"nm{par}_{h4}")[:], t(cm)[:, 0:nch],
                                           axis=AX.X, negate=True),
                              reads=[(cm, c) for c in range(nch)],
                              writes=[(f"nm{par}_{h4}", 0)])
                    yield
                    if prev_hs is not None:
                        ztail(prev_hs)
                        yield
                    for h4 in hs:
                        base, bank_keys = geom(h4)
                        pb = f"pbf{par}_{h4}"
                        zp = f"zp{par}_{h4}"
                        for c in range(nch):
                            kb.op("act", lambda e, base=base, c=c, w=widths[c],
                                  pb=pb, zp=zp, h4=h4, par=par:
                                  e.activation(
                                      t(pb)[:, c * 512: c * 512 + w],
                                      psv(base + c, 0, w), AF.Exp,
                                      bias=t(f"nm{par}_{h4}")[:], scale=1.0,
                                      accum_out=t(zp)[:, c:c + 1]),
                                  reads=[bank_keys[c], (f"nm{par}_{h4}", 0)],
                                  writes=[(pb, c), (zp, c)])
                        yield
                    prev_hs = hs
                ztail(prev_hs)
                yield

            def ptpv_g(qt, kvh):
                tlq = qt % 2
                par = (4 * qt + kvh) % 3
                ob = 6 + kvh % 2

                def trans(sblk):
                    ptb = 4 + sblk % 2
                    for h4 in range(4):
                        kb.op("pe", lambda e, h4=h4, sblk=sblk, ptb=ptb, par=par:
                              e.matmul(
                                  psv(ptb, h4 * 128, (h4 + 1) * 128),
                                  t(f"pbf{par}_{h4}")[:, sblk * 128:(sblk + 1) * 128],
                                  t(f"diag4_{par}")[:, h4 * 128:(h4 + 1) * 128],
                                  start=True, stop=True),
                              reads=[(f"pbf{par}_{h4}", (sblk * 128) // 512),
                                     (f"diag4_{par}", h4)],
                              writes=[("ps", ptb)])

                def cast(sblk):
                    ptb = 4 + sblk % 2
                    pt = f"pt4_{sblk % 2}"
                    if sblk % 2 == 0:
                        kb.op("dve", lambda e, ptb=ptb, pt=pt: e.tensor_copy(
                            t(pt)[:], psv(ptb)),
                            reads=[("ps", ptb)], writes=[(pt, 0)])
                    else:
                        kb.op("act", lambda e, ptb=ptb, pt=pt: e.activation(
                            t(pt)[:], psv(ptb), AF.Copy),
                            reads=[("ps", ptb)], writes=[(pt, 0)])

                def pv(sblk):
                    pt = f"pt4_{sblk % 2}"
                    kb.op("pe", lambda e, sblk=sblk, kvh=kvh, pt=pt, qt=qt, ob=ob:
                          e.matmul(
                              psv(ob),
                              t("vres")[:, sblk * KD + kvh * 128: sblk * KD + (kvh + 1) * 128],
                              t(pt)[:], start=(sblk == 0), stop=(sblk == qt)),
                          reads=[("vres", sblk), (pt, 0)], writes=[("ps", ob)])

                for sblk in range(qt + 1):
                    trans(sblk)
                    if sblk >= 1:
                        cast(sblk - 1)
                    if sblk >= 2:
                        pv(sblk - 2)
                    yield
                cast(qt)
                if qt >= 1:
                    pv(qt - 1)
                pv(qt)
                kb.op("act", lambda e, kvh=kvh, tlq=tlq, ob=ob: e.activation(
                    t("ytg")[:].rearrange("p (d x) -> p d x", x=256)[
                        :, kvh * 4:(kvh + 1) * 4, tlq * 128:(tlq + 1) * 128],
                    psv(ob).rearrange("p (d x) -> p d x", x=128),
                    AF.Copy),
                    reads=[("ps", ob)], writes=[("ytg", (tlq, kvh))])

            def ln_steps(insts):
                for c4 in range(4):
                    for rs_name, out_name, li in insts:
                        kb.op("dve", lambda e, c4=c4, rs_name=rs_name, li=li: e.bn_stats(
                            t(f"bn24_{li}")[:, c4 * 6:(c4 + 1) * 6],
                            t(rs_name)[:, c4 * 512:(c4 + 1) * 512]),
                            reads=[(rs_name, c4)],
                            writes=[(f"bn24_{li}", 0)])
                for rs_name, out_name, li in insts:
                    kb.op("dve", lambda e, li=li: e.reduce_sum(
                        t(f"summ_{li}")[:],
                        t(f"bn24_{li}")[:].rearrange("p (g s) -> p s g", s=3)[:, 1, :],
                        axis=AX.X),
                        reads=[(f"bn24_{li}", 0)], writes=[(f"summ_{li}", 0)])
                    kb.op("dve", lambda e, li=li: e.reduce_sum(
                        t(f"sM2_{li}")[:],
                        t(f"bn24_{li}")[:].rearrange("p (g s) -> p s g", s=3)[:, 2, :],
                        axis=AX.X),
                        reads=[(f"bn24_{li}", 0)], writes=[(f"sM2_{li}", 0)])
                for rs_name, out_name, li in insts:
                    kb.op("dve", lambda e, li=li: e.tensor_scalar_mul(
                        t(f"meant_{li}")[:], t(f"summ_{li}")[:], 0.125),
                        reads=[(f"summ_{li}", 0)], writes=[(f"meant_{li}", 0)])
                    kb.op("dve", lambda e, li=li: e.tensor_scalar(
                        t(f"dev_{li}")[:],
                        t(f"bn24_{li}")[:].rearrange("p (g s) -> p s g", s=3)[:, 1, :],
                        t(f"meant_{li}")[:], None, op0=ALU.subtract),
                        reads=[(f"bn24_{li}", 0), (f"meant_{li}", 0)],
                        writes=[(f"dev_{li}", 0)])
                for rs_name, out_name, li in insts:
                    kb.op("dve", lambda e, li=li: e.tensor_mul(
                        t(f"dev2_{li}")[:], t(f"dev_{li}")[:], t(f"dev_{li}")[:]),
                        reads=[(f"dev_{li}", 0)], writes=[(f"dev2_{li}", 0)])
                for rs_name, out_name, li in insts:
                    kb.op("dve", lambda e, li=li: e.reduce_sum(
                        t(f"sdev_{li}")[:], t(f"dev2_{li}")[:], axis=AX.X),
                        reads=[(f"dev2_{li}", 0)], writes=[(f"sdev_{li}", 0)])
                    kb.op("dve", lambda e, li=li: e.tensor_scalar(
                        t(f"sM2_{li}")[:], t(f"sM2_{li}")[:], 1.0 / D, EPS_LN,
                        op0=ALU.mult, op1=ALU.add),
                        reads=[(f"sM2_{li}", 0)], writes=[(f"sM2_{li}", 0)])
                for rs_name, out_name, li in insts:
                    kb.op("dve", lambda e, li=li: e.scalar_tensor_tensor(
                        t(f"vart_{li}")[:], t(f"sdev_{li}")[:], 0.125, t(f"sM2_{li}")[:],
                        op0=ALU.mult, op1=ALU.add),
                        reads=[(f"sdev_{li}", 0), (f"sM2_{li}", 0)],
                        writes=[(f"vart_{li}", 0)])
                for rs_name, out_name, li in insts:
                    kb.op("act", lambda e, li=li: e.activation(
                        t(f"lnv_{li}")[:], t(f"vart_{li}")[:], AF.Ln),
                        reads=[(f"vart_{li}", 0)], writes=[(f"lnv_{li}", 0)])
                for rs_name, out_name, li in insts:
                    kb.op("dve", lambda e, li=li: e.tensor_scalar_mul(
                        t(f"lnv_{li}")[:], t(f"lnv_{li}")[:], -0.5),
                        reads=[(f"lnv_{li}", 0)], writes=[(f"lnv_{li}", 0)])
                for rs_name, out_name, li in insts:
                    kb.op("act", lambda e, li=li: e.activation(
                        t(f"rstd1_{li}")[:], t(f"lnv_{li}")[:], AF.Exp),
                        reads=[(f"lnv_{li}", 0)], writes=[(f"rstd1_{li}", 0)])
                for rs_name, out_name, li in insts:
                    kb.op("dve", lambda e, li=li: e.tensor_scalar(
                        t(f"nmr_{li}")[:], t(f"meant_{li}")[:], t(f"rstd1_{li}")[:], -1.0,
                        op0=ALU.mult, op1=ALU.mult),
                        reads=[(f"meant_{li}", 0), (f"rstd1_{li}", 0)],
                        writes=[(f"nmr_{li}", 0)])
                for rs_name, out_name, li in insts:
                    kb.op("act", lambda e, rs_name=rs_name, out_name=out_name, li=li:
                          e.activation(
                              t(out_name)[:], t(rs_name)[:], AF.Identity,
                              bias=t(f"nmr_{li}")[:], scale=t(f"rstd1_{li}")[:]),
                          reads=[(rs_name, i) for i in range(4)] +
                                [(f"nmr_{li}", 0), (f"rstd1_{li}", 0)],
                          writes=[(out_name, 0)])

            def proj_ln_store_g(qt, seq=seq):
                tl = qt % 2
                rs = f"res{qt % 2}"
                x1 = f"x1bf{qt % 2}"
                xb = f"xbf{qt % 2}"
                li = qt % 2
                kb.op("act", lambda e, rs=rs, xb=xb: e.activation(
                    t(xb)[:], t(rs)[:], AF.Copy),
                    reads=[(rs, c) for c in range(4)], writes=[(xb, 0)])
                for q in range(4):
                    bank = 4 + q % 2
                    kb.op("pe", lambda e, q=q, bank=bank, xb=xb: e.matmul(
                        psv(bank), t("ident")[:],
                        t(xb)[:, q * 512:(q + 1) * 512],
                        start=True, stop=False),
                        reads=[("ident", 0), (xb, 0)], writes=[("ps", bank)])
                    for dc in range(16):
                        kb.op("pe", lambda e, dc=dc, q=q, tl=tl, bank=bank: e.matmul(
                            psv(bank),
                            t("ytg")[:, dc * 256 + tl * 128: dc * 256 + (tl + 1) * 128],
                            t("wp")[:, dc * D + q * 512: dc * D + (q + 1) * 512],
                            start=False, stop=(dc == 15)),
                            reads=[("ytg", (tl, k)) for k in range(4)] + [("wp", dc)],
                            writes=[("ps", bank)])
                    kb.op("act", lambda e, q=q, bank=bank, rs=rs: e.activation(
                        t(rs)[:, q * 512:(q + 1) * 512], psv(bank), AF.Copy),
                        reads=[("ps", bank)], writes=[(rs, q)])
                    yield
                ln_steps([(rs, x1, li)])
                kb.op("act", lambda e, seq=seq, qt=qt, x1=x1: e.dma_start(
                    out=dram["x1k"][seq, qt], in_=t(x1)[:]),
                    reads=[(x1, 0)], writes=[("x1k", seq * 16 + qt)], dma=True)

            def pull(Q):
                try:
                    next(Q[0][1])
                    return True
                except StopIteration:
                    Q.pop(0)
                    return False

            def interleave(sm_gen, Q, forbid=None):
                if forbid is not None:
                    while any(tag == forbid for tag, _ in Q):
                        pull(Q)
                for _ in sm_gen:
                    pulls = 2
                    while pulls > 0 and Q:
                        if pull(Q):
                            pulls -= 1

            def drain(Q, upto=None):
                while Q:
                    tag = Q[0][0]
                    if not pull(Q):
                        if upto is not None and tag == upto:
                            return

            qload(0)
            qload(1)
            xload(0)
            Q = []
            for qt in range(NTC):
                for kvh in range(4):
                    u = 4 * qt + kvh
                    if kvh == 1 and qt > 0:
                        Q.append(("proj", proj_ln_store_g(qt - 1)))
                    interleave(softmax_g(qt, kvh), Q, forbid=u % 3)
                    Q.append((u % 3, ptpv_g(qt, kvh)))
                    if kvh == 2:
                        if qt > 0:
                            drain(Q, upto="proj")
                        if qt + 1 < NTC:
                            xload(qt + 1)
                if qt + 2 < NTC:
                    qload(qt + 2)
            drain(Q)
            drain([("proj", proj_ln_store_g(NTC - 1))])
        kb.close_scope(s2)
        kb.close_scope(sc)

    kb.open_scope("s3")
    with kb.phase("s3"):
        kb.sbuf("x1tg0", [128, 4 * D], BF16, "s3")
        for i in range(4):
            kb.sbuf(f"w1h{i}", [128, 8 * 128], BF16, "s3")
            kb.sbuf(f"w3h{i}", [128, 16 * 128], BF16, "s3")
        for i in range(6):
            kb.sbuf(f"w2h{i}", [128, 16 * 128], BF16, "s3")
        for i in range(3):
            kb.sbuf(f"otbf{i}", [128, 512], BF16, "s3")
        kb.sbuf("x1ks0", [128, 4 * D], BF16, "s3")
        kb.sbuf("x1ks1", [128, 4 * D], BF16, "s3")
        kb.sbuf("h1", [128, 32 * 512], BF16, "s3")
        kb.sbuf("h2", [128, 32 * 512], BF16, "s3")
        for i in range(4):
            kb.sbuf(f"res2_{i}", [128, D], F32, "s3")

        def x1_load_transpose(gidx):
            seq, g = divmod(gidx, 4)
            xk = f"x1ks{gidx % 2}"
            xg = "x1tg0"
            for tcl in range(4):
                kb.op("sync", lambda e, seq=seq, g=g, tcl=tcl, xk=xk: e.dma_start(
                    out=t(xk)[:, tcl * D:(tcl + 1) * D],
                    in_=dram["x1k"][seq, g * 4 + tcl]),
                    reads=[("x1k", seq * 16 + g * 4 + tcl)], writes=[(xk, tcl)], dma=True)
            for tcl in range(4):
                for g2 in range(4):
                    bank = 6 + (tcl * 4 + g2) % 2
                    for j in range(4):
                        dc = g2 * 4 + j
                        kb.op("pe", lambda e, tcl=tcl, dc=dc, j=j, bank=bank, xk=xk: e.matmul(
                            psv(bank, j * 128, (j + 1) * 128),
                            t(xk)[:, tcl * D + dc * 128: tcl * D + (dc + 1) * 128],
                            t("ident")[:], start=True, stop=True),
                            reads=[(xk, tcl), ("ident", 0)], writes=[("ps", bank)])
                    kb.op("dve", lambda e, tcl=tcl, g2=g2, bank=bank, xg=xg: e.tensor_copy(
                        t(xg)[:, tcl * D + g2 * 512: tcl * D + (g2 + 1) * 512], psv(bank)),
                        reads=[("ps", bank)], writes=[(xg, tcl)])

        for gidx in range(8):
            seq, g = divmod(gidx, 4)
            xg = "x1tg0"
            xk = f"x1ks{gidx % 2}"
            if gidx == 0:
                x1_load_transpose(0)

            def xg_view(dc, xg=xg):
                return t(xg)[:].rearrange("p (c n) -> p c n", c=4)[:, :, dc * 128:(dc + 1) * 128]

            RING4 = ("sync", "pool")
            for hc in range(32):
                for hf in range(2):
                    idx = 2 * hc + hf
                    ws = f"w1h{idx % 4}"
                    kb.op(RING4[idx % 2], lambda e, hc=hc, hf=hf, ws=ws: e.dma_start(
                        out=t(ws)[:],
                        in_=dram["w1d"][hc][:, hf * 8:(hf + 1) * 8, :].rearrange(
                            "p c n -> p (c n)")),
                        reads=[("w1d", hc // 16)], writes=[(ws, 0)], dma=True)
                bank = hc % 4
                for dc in range(16):
                    ws = f"w1h{(2 * hc + dc // 8) % 4}"
                    kb.op("pe", lambda e, dc=dc, ws=ws, bank=bank, xg=xg: e.matmul(
                        psv(bank), t(ws)[:, (dc % 8) * 128:(dc % 8 + 1) * 128],
                        xg_view(dc, xg), start=(dc == 0), stop=(dc == 15)),
                        reads=[(ws, 0)] + [(xg, i) for i in range(4)],
                        writes=[("ps", bank)])
                kb.op("act", lambda e, hc=hc, bank=bank: e.activation(
                    t("h1")[:, hc * 512:(hc + 1) * 512], psv(bank), AF.Relu,
                    bias=t("b1c")[:, hc:hc + 1], scale=1.0),
                    reads=[("ps", bank), ("b1c", 0)], writes=[("h1", hc)])
            for jc in range(32):
                for hf in range(2):
                    idx = 2 * jc + hf
                    ws = f"w2h{idx % 6}"
                    kb.op(RING4[idx % 2], lambda e, jc=jc, hf=hf, ws=ws: e.dma_start(
                        out=t(ws)[:],
                        in_=dram["w2d"][jc][:, hf * 16:(hf + 1) * 16, :].rearrange(
                            "p c n -> p (c n)")),
                        reads=[("w2d", jc // 8)], writes=[(ws, 0)], dma=True)
                bank = jc % 4
                for hc in range(32):
                    ws = f"w2h{(2 * jc + hc // 16) % 6}"
                    kb.op("pe", lambda e, hc=hc, ws=ws, bank=bank: e.matmul(
                        psv(bank), t(ws)[:, (hc % 16) * 128:(hc % 16 + 1) * 128],
                        t("h1")[:, hc * 512:(hc + 1) * 512],
                        start=(hc == 0), stop=(hc == 31)),
                        reads=[(ws, 0), ("h1", hc)], writes=[("ps", bank)])
                kb.op("act", lambda e, jc=jc, bank=bank: e.activation(
                    t("h2")[:, jc * 512:(jc + 1) * 512], psv(bank), AF.Relu,
                    bias=t("b2c")[:, jc:jc + 1], scale=1.0),
                    reads=[("ps", bank), ("b2c", 0)], writes=[("h2", jc)])
            if gidx + 1 < 8:
                x1_load_transpose(gidx + 1)
            for oc in range(16):
                ob = f"otbf{oc % 2}"
                for hf in range(2):
                    idx = 2 * oc + hf
                    ws = f"w3h{idx % 4}"
                    kb.op(RING4[idx % 2], lambda e, oc=oc, hf=hf, ws=ws: e.dma_start(
                        out=t(ws)[:],
                        in_=dram["w3d"][oc][:, hf * 16:(hf + 1) * 16, :].rearrange(
                            "p c n -> p (c n)")),
                        reads=[("w3d", oc // 8)], writes=[(ws, 0)], dma=True)
                bank = oc % 2
                for hc in range(32):
                    ws = f"w3h{(2 * oc + hc // 16) % 4}"
                    kb.op("pe", lambda e, hc=hc, ws=ws, bank=bank: e.matmul(
                        psv(bank), t(ws)[:, (hc % 16) * 128:(hc % 16 + 1) * 128],
                        t("h2")[:, hc * 512:(hc + 1) * 512],
                        start=(hc == 0), stop=(hc == 31)),
                        reads=[(ws, 0), ("h2", hc)], writes=[("ps", bank)])
                kb.op("act", lambda e, oc=oc, bank=bank, ob=ob: e.activation(
                    t(ob)[:], psv(bank), AF.Identity,
                    bias=t("b3c")[:, oc:oc + 1], scale=1.0),
                    reads=[("ps", bank), ("b3c", 0)], writes=[(ob, 0)])
                tbank = 4 + oc % 2
                for tl in range(4):
                    kb.op("pe", lambda e, tl=tl, ob=ob, tbank=tbank: e.matmul(
                        psv(tbank, tl * 128, (tl + 1) * 128),
                        t(ob)[:, tl * 128:(tl + 1) * 128], t("ident")[:],
                        start=True, stop=True),
                        reads=[(ob, 0), ("ident", 0)], writes=[("ps", tbank)])
                for tl in range(4):
                    kb.op("dve", lambda e, tl=tl, oc=oc, tbank=tbank, xk=xk: e.tensor_add(
                        t(f"res2_{tl}")[:, oc * 128:(oc + 1) * 128],
                        psv(tbank, tl * 128, (tl + 1) * 128),
                        t(xk)[:, tl * D + oc * 128: tl * D + (oc + 1) * 128]),
                        reads=[("ps", tbank), (xk, tl)],
                        writes=[(f"res2_{tl}", oc), (f"res2_{tl}", "ln")])
            insts = [(f"res2_{tl}", None, tl) for tl in range(4)]
            for c4 in range(4):
                for rs_name, out_name, li in insts:
                    kb.op("dve", lambda e, c4=c4, rs_name=rs_name, li=li: e.bn_stats(
                        t(f"bn24_{li}")[:, c4 * 6:(c4 + 1) * 6],
                        t(rs_name)[:, c4 * 512:(c4 + 1) * 512]),
                        reads=[(rs_name, i) for i in range(16)], writes=[(f"bn24_{li}", 0)])
            for rs_name, out_name, li in insts:
                kb.op("dve", lambda e, li=li: e.reduce_sum(
                    t(f"summ_{li}")[:],
                    t(f"bn24_{li}")[:].rearrange("p (g s) -> p s g", s=3)[:, 1, :],
                    axis=AX.X),
                    reads=[(f"bn24_{li}", 0)], writes=[(f"summ_{li}", 0)])
                kb.op("dve", lambda e, li=li: e.reduce_sum(
                    t(f"sM2_{li}")[:],
                    t(f"bn24_{li}")[:].rearrange("p (g s) -> p s g", s=3)[:, 2, :],
                    axis=AX.X),
                    reads=[(f"bn24_{li}", 0)], writes=[(f"sM2_{li}", 0)])
            for rs_name, out_name, li in insts:
                kb.op("dve", lambda e, li=li: e.tensor_scalar_mul(
                    t(f"meant_{li}")[:], t(f"summ_{li}")[:], 0.125),
                    reads=[(f"summ_{li}", 0)], writes=[(f"meant_{li}", 0)])
                kb.op("dve", lambda e, li=li: e.tensor_scalar(
                    t(f"dev_{li}")[:],
                    t(f"bn24_{li}")[:].rearrange("p (g s) -> p s g", s=3)[:, 1, :],
                    t(f"meant_{li}")[:], None, op0=ALU.subtract),
                    reads=[(f"bn24_{li}", 0), (f"meant_{li}", 0)], writes=[(f"dev_{li}", 0)])
            for rs_name, out_name, li in insts:
                kb.op("dve", lambda e, li=li: e.tensor_mul(
                    t(f"dev2_{li}")[:], t(f"dev_{li}")[:], t(f"dev_{li}")[:]),
                    reads=[(f"dev_{li}", 0)], writes=[(f"dev2_{li}", 0)])
                kb.op("dve", lambda e, li=li: e.reduce_sum(
                    t(f"sdev_{li}")[:], t(f"dev2_{li}")[:], axis=AX.X),
                    reads=[(f"dev2_{li}", 0)], writes=[(f"sdev_{li}", 0)])
                kb.op("dve", lambda e, li=li: e.tensor_scalar(
                    t(f"sM2_{li}")[:], t(f"sM2_{li}")[:], 1.0 / D, EPS_LN,
                    op0=ALU.mult, op1=ALU.add),
                    reads=[(f"sM2_{li}", 0)], writes=[(f"sM2_{li}", 0)])
            for rs_name, out_name, li in insts:
                kb.op("dve", lambda e, li=li: e.scalar_tensor_tensor(
                    t(f"vart_{li}")[:], t(f"sdev_{li}")[:], 0.125, t(f"sM2_{li}")[:],
                    op0=ALU.mult, op1=ALU.add),
                    reads=[(f"sdev_{li}", 0), (f"sM2_{li}", 0)], writes=[(f"vart_{li}", 0)])
            for rs_name, out_name, li in insts:
                kb.op("act", lambda e, li=li: e.activation(
                    t(f"lnv_{li}")[:], t(f"vart_{li}")[:], AF.Ln),
                    reads=[(f"vart_{li}", 0)], writes=[(f"lnv_{li}", 0)])
            for rs_name, out_name, li in insts:
                kb.op("dve", lambda e, li=li: e.tensor_scalar_mul(
                    t(f"lnv_{li}")[:], t(f"lnv_{li}")[:], -0.5),
                    reads=[(f"lnv_{li}", 0)], writes=[(f"lnv_{li}", 0)])
            for rs_name, out_name, li in insts:
                kb.op("act", lambda e, li=li: e.activation(
                    t(f"rstd1_{li}")[:], t(f"lnv_{li}")[:], AF.Exp),
                    reads=[(f"lnv_{li}", 0)], writes=[(f"rstd1_{li}", 0)])
            for rs_name, out_name, li in insts:
                kb.op("dve", lambda e, li=li: e.tensor_scalar(
                    t(f"nmr_{li}")[:], t(f"meant_{li}")[:], t(f"rstd1_{li}")[:], -1.0,
                    op0=ALU.mult, op1=ALU.mult),
                    reads=[(f"meant_{li}", 0), (f"rstd1_{li}", 0)], writes=[(f"nmr_{li}", 0)])
            for rs_name, out_name, li in insts:
                tl = li
                kb.op("act", lambda e, rs_name=rs_name, li=li: e.activation(
                    t(rs_name)[:], t(rs_name)[:], AF.Identity,
                    bias=t(f"nmr_{li}")[:], scale=t(f"rstd1_{li}")[:]),
                    reads=[(rs_name, i) for i in range(16)] +
                          [(f"nmr_{li}", 0), (f"rstd1_{li}", 0)],
                    writes=[(rs_name, "ln")])
                kb.op("act", lambda e, seq=seq, g=g, tl=tl, rs_name=rs_name: e.dma_start(
                    out=dram["y"][seq, (g * 4 + tl) * 128:(g * 4 + tl + 1) * 128, :],
                    in_=t(rs_name)[:]),
                    reads=[(rs_name, "ln")], writes=[("ydone", gidx * 4 + tl)], dma=True)
    kb.close_scope("s3")

    with kb.phase("fin"):
        def fin(e):
            tot = kb.totals()
            for semname, val in tot.items():
                if val > 0 and not semname.startswith("qsync"):
                    e.wait_ge(kb.sems[semname], val)
            for semname, val in tot.items():
                if val > 0 and semname.startswith("qsync"):
                    e.wait_ge(kb.sems[semname], val)
            return None
        kb.op("sync", fin)
    kb.close_scope("glob")


def _host_prep(inputs):
    f32 = np.float32
    x = np.ascontiguousarray(inputs["L0_attn_x"], dtype=f32)
    wq = np.asarray(inputs["L0_attn_wq"], dtype=f32)
    wk = np.asarray(inputs["L0_attn_wk"], dtype=f32)
    wv = np.asarray(inputs["L0_attn_wv"], dtype=f32)
    wp = np.asarray(inputs["L0_attn_wp"], dtype=f32)
    qg = np.asarray(inputs["L0_attn_qg"], dtype=f32)
    W1 = np.asarray(inputs["L0_ffn_W1"], dtype=f32)
    b1 = np.asarray(inputs["L0_ffn_b1"], dtype=f32)
    W2 = np.asarray(inputs["L0_ffn_W2"], dtype=f32)
    b2 = np.asarray(inputs["L0_ffn_b2"], dtype=f32)
    W3 = np.asarray(inputs["L0_ffn_W3"], dtype=f32)
    b3 = np.asarray(inputs["L0_ffn_b3"], dtype=f32)

    shared = {}
    shared["wqt"] = np.ascontiguousarray(wq.T)
    shared["wkt"] = np.ascontiguousarray(wk.T)
    shared["wvt"] = np.ascontiguousarray(wv.T)
    shared["wpt"] = np.ascontiguousarray(wp.T)
    W1T = W1.T
    W2T = W2.T
    W3T = W3.T
    shared["w1b"] = np.ascontiguousarray(
        W1T.reshape(16, 128, 32, 128).transpose(2, 1, 0, 3))
    shared["w2b"] = np.ascontiguousarray(
        W2T.reshape(32, 128, 32, 128).transpose(2, 1, 0, 3))
    shared["w3b"] = np.ascontiguousarray(
        W3T.reshape(32, 128, 16, 128).transpose(2, 1, 0, 3))
    shared["b1cd"] = np.ascontiguousarray(b1.reshape(32, 128).T)
    shared["b2cd"] = np.ascontiguousarray(b2.reshape(32, 128).T)
    shared["b3cd"] = np.ascontiguousarray(b3.reshape(16, 128).T)
    shared["qgi"] = np.ascontiguousarray(qg.reshape(1, 16))
    shared["identi"] = np.eye(128, dtype=ml_dtypes.bfloat16)
    shared["maski"] = np.triu(np.full((128, 128), NEG_BIG, dtype=f32), k=1)
    shared["masktb"] = np.ascontiguousarray(
        np.tril(np.full((128, 128), NEG_BIG), k=-1)).astype(ml_dtypes.bfloat16)
    inv = (1.0 / 10000.0 ** (np.arange(0, PD, 2, dtype=f32) / PD)).astype(f32)
    f = np.arange(T, dtype=f32)[:, None] * inv[None, :]
    cosf = np.cos(f).astype(f32)
    sinf = np.sin(f).astype(f32)
    cost = np.tile(cosf, (1, 16))
    sint = np.tile(sinf, (1, 16))
    shared["cosbd"] = np.ascontiguousarray(
        cost.reshape(16, 128, 128).transpose(1, 0, 2).reshape(128, T))
    shared["sinbd"] = np.ascontiguousarray(
        sint.reshape(16, 128, 128).transpose(1, 0, 2).reshape(128, T))

    in_maps = []
    for c in range(NCORES):
        xs = x[2 * c:2 * c + 2]
        m = dict(shared)
        m["x"] = xs
        m["xtb"] = np.ascontiguousarray(
            xs.reshape(2, 16, 128, 16, 128).transpose(0, 1, 4, 3, 2).reshape(
                2, 16, 128, 2048))
        in_maps.append(m)
    return in_maps


def _build_program():
    nc = bass.Bass("TRN2", target_bir_lowering=False, debug=False)
    dram = {}
    dram["x"] = nc.dram_tensor("x", [2, T, D], F32, kind="ExternalInput").ap()
    dram["xtb"] = nc.dram_tensor("xtb", [2, NTC, 128, D], F32, kind="ExternalInput").ap()
    dram["wqt"] = nc.dram_tensor("wqt", [D, D], F32, kind="ExternalInput").ap()
    dram["wkt"] = nc.dram_tensor("wkt", [D, KD], F32, kind="ExternalInput").ap()
    dram["wvt"] = nc.dram_tensor("wvt", [D, KD], F32, kind="ExternalInput").ap()
    dram["wpt"] = nc.dram_tensor("wpt", [D, D], F32, kind="ExternalInput").ap()
    dram["w1b"] = nc.dram_tensor("w1b", [32, 128, 16, 128], F32, kind="ExternalInput").ap()
    dram["w2b"] = nc.dram_tensor("w2b", [32, 128, 32, 128], F32, kind="ExternalInput").ap()
    dram["w3b"] = nc.dram_tensor("w3b", [16, 128, 32, 128], F32, kind="ExternalInput").ap()
    dram["b1c"] = nc.dram_tensor("b1cd", [128, 32], F32, kind="ExternalInput").ap()
    dram["b2c"] = nc.dram_tensor("b2cd", [128, 32], F32, kind="ExternalInput").ap()
    dram["b3c"] = nc.dram_tensor("b3cd", [128, 16], F32, kind="ExternalInput").ap()
    dram["qgi"] = nc.dram_tensor("qgi", [1, 16], F32, kind="ExternalInput").ap()
    dram["identi"] = nc.dram_tensor("identi", [128, 128], BF16, kind="ExternalInput").ap()
    dram["maski"] = nc.dram_tensor("maski", [128, 128], F32, kind="ExternalInput").ap()
    dram["masktb"] = nc.dram_tensor("masktb", [128, 128], BF16, kind="ExternalInput").ap()
    dram["cosb"] = nc.dram_tensor("cosbd", [128, T], F32, kind="ExternalInput").ap()
    dram["sinb"] = nc.dram_tensor("sinbd", [128, T], F32, kind="ExternalInput").ap()
    dram["qtd"] = nc.dram_tensor("qtd", [2, NTC, 128, D], BF16, kind="Internal").ap()
    dram["x1t"] = nc.dram_tensor("x1td", [2, NTC, 128, D], BF16, kind="Internal").ap()
    dram["x1k"] = nc.dram_tensor("x1kd", [2, NTC, 128, D], BF16, kind="Internal").ap()
    dram["wpd"] = nc.dram_tensor("wpd", [D, D], BF16, kind="Internal").ap()
    dram["w1d"] = nc.dram_tensor("w1d", [32, 128, 16, 128], BF16, kind="Internal").ap()
    dram["w2d"] = nc.dram_tensor("w2d", [32, 128, 32, 128], BF16, kind="Internal").ap()
    dram["w3d"] = nc.dram_tensor("w3d", [16, 128, 32, 128], BF16, kind="Internal").ap()
    dram["y"] = nc.dram_tensor("y", [2, T, D], F32, kind="ExternalOutput").ap()

    kb = KB(nc)
    build_all(kb, dram)
    kb.finish_plan()
    with ExitStack() as st:
        kb.make_sems(st)
        build_all(kb, dram)
    for s in KB.STREAMS:
        assert kb.pos[s] == len(kb.ops[s]), (s, kb.pos[s], len(kb.ops[s]))
    return nc


def kernel(**inputs):
    in_maps = _host_prep(inputs)
    nc = _build_program()
    res = run_bass_kernel_spmd(nc, in_maps, core_ids=list(range(NCORES)))
    outs = [np.asarray(r["y"], dtype=np.float32) for r in res.results]
    return np.concatenate(outs, axis=0)


if __name__ == "__main__":
    nc = bass.Bass("TRN2", target_bir_lowering=False)
    dram = {k: None for k in ()}
    kb = KB(nc)

    class _D(dict):
        def __getitem__(self, k):
            return FakeAP()

    class FakeAP:
        shape = (32, 128, 16, 128)

        def __getitem__(self, *_):
            return self

        def rearrange(self, *a, **k):
            return self

    build_all(kb, _D())
    for s in KB.STREAMS:
        nd = sum(1 for r in kb.ops[s] if r.dma)
        nn = sum(1 for r in kb.ops[s] if r.needed)
        print(f"{s}: ops={len(kb.ops[s])} dma={nd} milestones={nn}")
    print("sbuf bytes/partition by scope:", {k: v for k, v in kb.sbuf_bytes.items()})


# version 48
# speedup vs baseline: 1.2328x; 1.0835x over previous
import numpy as np
import ml_dtypes
from contextlib import ExitStack, contextmanager

import concourse.bass as bass
import concourse.mybir as mybir
from concourse.bass_utils import run_bass_kernel_spmd

F32 = mybir.dt.float32
BF16 = mybir.dt.bfloat16
AF = mybir.ActivationFunctionType
ALU = mybir.AluOpType
AX = mybir.AxisListType

NCORES = 8
T = 2048
D = 2048
NTC = 16
NH, NKV, HD, PD = 16, 4, 128, 16
KD = NKV * HD
H1 = 4096
EPS_RMS = float(np.finfo(np.float32).eps)
EPS_LN = 1e-5
RSQRT_HD = 1.0 / float(np.sqrt(HD))
NEG_BIG = -1.0e30


class _Op:
    __slots__ = ("emit", "dma", "deps", "needed", "ms", "dma_ord")

    def __init__(self, dma, deps):
        self.emit = None
        self.dma = dma
        self.deps = deps
        self.needed = False
        self.ms = 0
        self.dma_ord = -1


class KB:
    STREAMS = ("pe", "act", "dve", "pool", "sync")
    ENG_OF = dict(pe="tensor", act="scalar", dve="vector", pool="gpsimd", sync="sync")
    NR = 16

    def __init__(self, nc):
        self.nc = nc
        self.mode = "plan"
        self.ops = {s: [] for s in self.STREAMS}
        self.pos = {s: 0 for s in self.STREAMS}
        self.dma_count = {s: 0 for s in self.STREAMS}
        self.buf = {}
        self.T = {}
        self.scope_stacks = {}
        self.scope_tiles = {}
        self.fence_due = {s: [] for s in self.STREAMS}
        self.highwater = {s: {} for s in self.STREAMS}
        self.sems = {}
        self.sbuf_bytes = {}

    def sbuf(self, name, shape, dtype, scope):
        if self.mode == "plan":
            esz = 2 if dtype == BF16 else 4
            self.sbuf_bytes[scope] = self.sbuf_bytes.get(scope, 0) + shape[-1] * esz
            self.scope_tiles.setdefault(scope, []).append(name)
            return name
        t = self.scope_stacks[scope].enter_context(
            self.nc.sbuf_tensor(f"{name}_{scope}", list(shape), dtype))
        self.T[name] = t
        return name

    def t(self, name):
        return self.T[name]

    def open_scope(self, name):
        if self.mode == "plan":
            self.scope_tiles.setdefault(name, [])
        else:
            self.scope_stacks[name] = ExitStack()

    def close_scope(self, name):
        if self.mode == "plan":
            handles = set()
            names = set(self.scope_tiles.get(name, ()))
            for key, ent in self.buf.items():
                if key[0] in names:
                    if ent[0] is not None:
                        handles.add(ent[0])
                    handles.update(ent[1])
            for s in self.STREAMS:
                self.fence_due[s].extend(handles)
        else:
            self.scope_stacks[name].close()
            del self.scope_stacks[name]

    def op(self, stream, emit, reads=(), writes=(), dma=False):
        if self.mode == "plan":
            handle = (stream, len(self.ops[stream]))
            deps = {}

            def add(h):
                if h is None:
                    return
                s, i = h
                if s == stream:
                    if s == "pe":
                        return
                    if dma and self.ops[s][i].dma:
                        return
                    if (not dma and not self.ops[s][i].dma
                            and len(self.ops[s]) - i > 4):
                        return
                deps[(s, i)] = True

            for h in self.fence_due[stream]:
                add(h)
            self.fence_due[stream] = []
            for k in reads:
                ent = self.buf.get(k)
                if ent is not None and ent[0] is not None:
                    add(ent[0])
            for k in writes:
                ent = self.buf.get(k)
                if ent is not None:
                    if ent[0] is not None:
                        add(ent[0])
                    for r in ent[1]:
                        add(r)
            best = {}
            for (s, i) in deps:
                o = self.ops[s][i]
                kd = (s, "d", (o.dma_ord - 1) % self.NR) if o.dma else (s, "c")
                if kd not in best or best[kd] < i:
                    best[kd] = i
            dl = sorted({(k[0], v) for k, v in best.items()})
            rec = _Op(dma, dl)
            if dma:
                self.dma_count[stream] += 1
                rec.dma_ord = self.dma_count[stream]
            self.ops[stream].append(rec)
            for (s, i) in dl:
                tgt = self.ops[s][i]
                if not tgt.dma:
                    tgt.needed = True
            for k in reads:
                self.buf.setdefault(k, [None, []])[1].append(handle)
            for k in writes:
                self.buf[k] = [handle, []]
            return handle
        else:
            i = self.pos[stream]
            self.pos[stream] += 1
            rec = self.ops[stream][i]
            assert rec.dma == dma, f"plan/emit divergence on {stream}[{i}]"
            rec.emit = emit
            return (stream, i)

    def finish_plan(self):
        for s in self.STREAMS:
            ms = 0
            ordn = 0
            for rec in self.ops[s]:
                if rec.dma:
                    ordn += 1
                    rec.dma_ord = ordn
                elif rec.needed:
                    ms += 1
                    rec.ms = ms
        self.buf = {}
        self.fence_due = {s: [] for s in self.STREAMS}
        self.mode = "emit"

    def make_sems(self, stack):
        for s in self.STREAMS:
            self.sems["s" + s] = stack.enter_context(self.nc.semaphore("s" + s))
            for r in range(self.NR):
                self.sems[f"q{s}_{r}"] = stack.enter_context(
                    self.nc.semaphore(f"q{s}_{r}"))

    def waitval(self, handle):
        s, i = handle
        rec = self.ops[s][i]
        if rec.dma:
            o = rec.dma_ord - 1
            return (f"q{s}_{o % self.NR}", 16 * (o // self.NR + 1))
        return ("s" + s, rec.ms)

    def totals(self):
        out = {}
        for s in self.STREAMS:
            nd = sum(1 for r in self.ops[s] if r.dma)
            nm = max([r.ms for r in self.ops[s] if not r.dma], default=0)
            for r in range(self.NR):
                cnt = (nd - r + self.NR - 1) // self.NR if nd > r else 0
                out[f"q{s}_{r}"] = 16 * cnt
            out["s" + s] = nm
        return out

    @contextmanager
    def phase(self, name):
        if self.mode == "plan":
            yield
            return
        start = {s: self.pos[s] for s in self.STREAMS}
        yield
        end = {s: self.pos[s] for s in self.STREAMS}
        with self.nc.Block(name) as blk:
            for s in self.STREAMS:
                if end[s] == start[s]:
                    continue
                dec = getattr(blk, self.ENG_OF[s])

                def body(eng, s=s, a=start[s], b=end[s]):
                    hw = self.highwater[s]
                    for i in range(a, b):
                        rec = self.ops[s][i]
                        for dep in rec.deps:
                            sem, val = self.waitval(dep)
                            if hw.get(sem, 0) < val:
                                eng.wait_ge(self.sems[sem], val)
                                hw[sem] = val
                        inst = rec.emit(eng)
                        if inst is None:
                            continue
                        if rec.dma:
                            inst.then_inc(
                                self.sems[f"q{s}_{(rec.dma_ord - 1) % self.NR}"], 16)
                        elif rec.needed:
                            inst.then_inc(self.sems["s" + s], 1)

                dec(body)


def build_all(kb, dram):
    nc = kb.nc
    t = kb.t
    PS = [f"ps{i}" for i in range(8)]

    def psv(i, a=0, b=512):
        return t(PS[i])[:, a:b]

    def score_geom(qt):
        span = (qt + 1) * 128
        nch = (span + 511) // 512
        widths = [min(512, span - 512 * c) for c in range(nch)]
        off = widths[-1] - 128
        return span, nch, widths, off

    kb.open_scope("glob")
    with kb.phase("setup"):
        for i in range(8):
            if kb.mode == "plan":
                kb.scope_tiles.setdefault("glob", []).append(PS[i])
            else:
                kb.T[PS[i]] = kb.scope_stacks["glob"].enter_context(
                    nc.psum_tensor(PS[i], [128, 512], F32))
        kb.sbuf("ident", [128, 128], BF16, "glob")
        kb.sbuf("maskt", [128, 128], F32, "glob")
        kb.sbuf("maskb", [128, 128], BF16, "glob")
        kb.sbuf("b1c", [128, 32], F32, "glob")
        kb.sbuf("b2c", [128, 32], F32, "glob")
        kb.sbuf("b3c", [128, 16], F32, "glob")
        kb.sbuf("qgsb", [1, 16], F32, "glob")
        kb.sbuf("ones1", [1, 128], F32, "glob")
        names = [("qgs", 16)]
        for p in range(3):
            for h4 in range(4):
                names += [(f"nm{p}_{h4}", 1), (f"zz{p}_{h4}", 1), (f"rz{p}_{h4}", 1),
                          (f"cmx{p}_{h4}", 4), (f"zp{p}_{h4}", 4)]
        for i in range(4):
            names += [(f"bn24_{i}", 24), (f"summ_{i}", 1), (f"sM2_{i}", 1),
                      (f"meant_{i}", 1), (f"dev_{i}", 8), (f"dev2_{i}", 8),
                      (f"sdev_{i}", 1), (f"vart_{i}", 1), (f"lnv_{i}", 1),
                      (f"rstd1_{i}", 1), (f"nmr_{i}", 1)]
        for p in range(2):
            names += [(f"ssq{p}", 16), (f"ssk{p}", 4), (f"rqraw{p}", 16),
                      (f"rkraw{p}", 4), (f"rstdq{p}", 16), (f"lnt{p}", 20)]
        for nm, w in names:
            kb.sbuf(nm, [128, w], F32, "glob")

        kb.op("sync", lambda e: e.dma_start(out=t("ident")[:], in_=dram["identi"]),
              writes=[("ident", 0)], dma=True)
        kb.op("sync", lambda e: e.dma_start(out=t("maskt")[:], in_=dram["maski"]),
              writes=[("maskt", 0)], dma=True)
        kb.op("sync", lambda e: e.dma_start(out=t("maskb")[:], in_=dram["masktb"]),
              writes=[("maskb", 0)], dma=True)
        kb.op("sync", lambda e: e.dma_start(out=t("b1c")[:], in_=dram["b1c"]),
              writes=[("b1c", 0)], dma=True)
        kb.op("sync", lambda e: e.dma_start(out=t("b2c")[:], in_=dram["b2c"]),
              writes=[("b2c", 0)], dma=True)
        kb.op("sync", lambda e: e.dma_start(out=t("b3c")[:], in_=dram["b3c"]),
              writes=[("b3c", 0)], dma=True)
        kb.op("sync", lambda e: e.dma_start(out=t("qgsb")[:], in_=dram["qgi"]),
              writes=[("qgsb", 0)], dma=True)
        kb.op("pool", lambda e: e.memset(t("ones1")[:], 1.0), writes=[("ones1", 0)])
        kb.op("pe", lambda e: e.matmul(psv(7, 0, 16), t("ones1")[:], t("qgsb")[:],
                                       start=True, stop=True),
              reads=[("ones1", 0), ("qgsb", 0)], writes=[("ps", 7)])
        kb.op("dve", lambda e: e.tensor_scalar_mul(t("qgs")[:], psv(7, 0, 16), RSQRT_HD),
              reads=[("ps", 7)], writes=[("qgs", 0)])
        def emit_w_casts():
            for nm, srcn, dstn, parts in (("w1d", "w1b", "w1d", 2), ("w2d", "w2b", "w2d", 4),
                                          ("w3d", "w3b", "w3d", 2)):
                n0 = dram[srcn].shape[0]
                step = n0 // parts
                for p in range(parts):
                    kb.op("pool",
                          lambda e, srcn=srcn, dstn=dstn, p=p, step=step:
                              e.dma_start(out=dram[dstn][p * step:(p + 1) * step],
                                          in_=dram[srcn][p * step:(p + 1) * step]),
                          writes=[(nm, p)], dma=True)
        kb.emit_w_casts = emit_w_casts

    for seq in range(2):
        sc = f"seq{seq}"
        kb.open_scope(sc)
        s1 = f"s1_{seq}"
        kb.open_scope(s1)
        with kb.phase(s1):
            kb.sbuf("kT", [128, NKV * T], BF16, sc)
            kb.sbuf("vres", [128, NTC * KD], BF16, sc)
            kb.sbuf("wq", [128, 16 * D], BF16, s1)
            kb.sbuf("wk", [128, 16 * KD], BF16, s1)
            kb.sbuf("wv", [128, 16 * KD], BF16, s1)
            kb.sbuf("cosb", [128, T], F32, s1)
            kb.sbuf("sinb", [128, T], F32, s1)
            for i in range(2):
                kb.sbuf(f"xt{i}", [128, D], BF16, s1)
                kb.sbuf(f"qtc{i}", [128, D], BF16, s1)
                kb.sbuf(f"qtok{i}", [128, D], F32, s1)
                kb.sbuf(f"qbf{i}", [128, D], BF16, s1)
            kb.sbuf("ktok0", [128, KD], F32, s1)
            kb.sbuf("kbf0", [128, KD], BF16, s1)
            kb.sbuf("kbf1", [128, KD], BF16, s1)
            kb.sbuf("scr0", [128, D], F32, s1)
            kb.sbuf("scrk0", [128, KD], F32, s1)
            kb.sbuf("tm1", [128, 128], F32, s1)
            kb.sbuf("tm2", [128, 128], F32, s1)
            kb.sbuf("tm3", [128, 128], F32, s1)
            kb.sbuf("tm4", [128, 128], F32, s1)

            def xload_s1(tc, seq=seq):
                xt = f"xt{tc % 2}"
                kb.op("pool", lambda e, seq=seq, tc=tc, xt=xt: e.dma_start(
                    out=t(xt)[:], in_=dram["xtb"][seq, tc]),
                    writes=[(xt, 0)], dma=True)

            xload_s1(0)
            xload_s1(1)
            for dc in range(16):
                kb.op("pool", lambda e, dc=dc: e.dma_start(
                    out=t("wq")[:, dc * D:(dc + 1) * D],
                    in_=dram["wqt"][dc * 128:(dc + 1) * 128, :]),
                    writes=[("wq", dc)], dma=True)
                kb.op("pool", lambda e, dc=dc: e.dma_start(
                    out=t("wk")[:, dc * KD:(dc + 1) * KD],
                    in_=dram["wkt"][dc * 128:(dc + 1) * 128, :]),
                    writes=[("wk", dc)], dma=True)
                kb.op("pool", lambda e, dc=dc: e.dma_start(
                    out=t("wv")[:, dc * KD:(dc + 1) * KD],
                    in_=dram["wvt"][dc * 128:(dc + 1) * 128, :]),
                    writes=[("wv", dc)], dma=True)
            kb.op("sync", lambda e: e.dma_start(out=t("cosb")[:], in_=dram["cosb"]),
                  writes=[("cosb", 0)], dma=True)
            kb.op("sync", lambda e: e.dma_start(out=t("sinb")[:], in_=dram["sinb"]),
                  writes=[("sinb", 0)], dma=True)

            def s1_front(tc, seq=seq):
                bi = tc % 2
                xt = f"xt{bi}"
                qtok = f"qtok{bi}"
                ktok = "ktok0"
                qbf = f"qbf{bi}"
                kbf = f"kbf{bi}"
                for dc in range(16):
                    for q in range(4):
                        kb.op("pe", lambda e, xt=xt, dc=dc, q=q: e.matmul(
                            psv(q), t(xt)[:, dc * 128:(dc + 1) * 128],
                            t("wq")[:, dc * D + q * 512: dc * D + (q + 1) * 512],
                            start=(dc == 0), stop=(dc == 15)),
                            reads=[(xt, 0), ("wq", dc)], writes=[("ps", q)])
                    kb.op("pe", lambda e, xt=xt, dc=dc: e.matmul(
                        psv(4), t(xt)[:, dc * 128:(dc + 1) * 128],
                        t("wk")[:, dc * KD:(dc + 1) * KD],
                        start=(dc == 0), stop=(dc == 15)),
                        reads=[(xt, 0), ("wk", dc)], writes=[("ps", 4)])
                    kb.op("pe", lambda e, xt=xt, dc=dc: e.matmul(
                        psv(5), t(xt)[:, dc * 128:(dc + 1) * 128],
                        t("wv")[:, dc * KD:(dc + 1) * KD],
                        start=(dc == 0), stop=(dc == 15)),
                        reads=[(xt, 0), ("wv", dc)], writes=[("ps", 5)])
                for q in range(4):
                    kb.op("act", lambda e, q=q, qtok=qtok: e.activation(
                        t(qtok)[:, q * 512:(q + 1) * 512], psv(q), AF.Copy),
                        reads=[("ps", q)], writes=[(qtok, 0)])
                kb.op("act", lambda e, ktok=ktok: e.activation(t(ktok)[:], psv(4), AF.Copy),
                      reads=[("ps", 4)], writes=[(ktok, 0)])
                kb.op("act", lambda e, tc=tc: e.activation(
                    t("vres")[:, tc * KD:(tc + 1) * KD], psv(5), AF.Copy),
                    reads=[("ps", 5)], writes=[("vres", tc)])
                scr, scrk = "scr0", "scrk0"
                ssq, ssk = f"ssq{bi}", f"ssk{bi}"
                lnt, rqraw, rkraw, rstdq = f"lnt{bi}", f"rqraw{bi}", f"rkraw{bi}", f"rstdq{bi}"
                kb.op("act", lambda e, qtok=qtok, scr=scr: e.activation(t(scr)[:], t(qtok)[:], AF.Square),
                      reads=[(qtok, 0)], writes=[(scr, 0)])
                kb.op("act", lambda e, ktok=ktok, scrk=scrk: e.activation(t(scrk)[:], t(ktok)[:], AF.Square),
                      reads=[(ktok, 0)], writes=[(scrk, 0)])
                kb.op("dve", lambda e, scr=scr, ssq=ssq: e.reduce_sum(
                    t(ssq)[:], t(scr)[:].rearrange("p (h j) -> p h j", j=128), axis=AX.X),
                    reads=[(scr, 0)], writes=[(ssq, 0)])
                kb.op("dve", lambda e, scrk=scrk, ssk=ssk: e.reduce_sum(
                    t(ssk)[:], t(scrk)[:].rearrange("p (h j) -> p h j", j=128), axis=AX.X),
                    reads=[(scrk, 0)], writes=[(ssk, 0)])
                kb.op("dve", lambda e, ssq=ssq: e.tensor_scalar(
                    t(ssq)[:], t(ssq)[:], 1.0 / HD, EPS_RMS, op0=ALU.mult, op1=ALU.add),
                    reads=[(ssq, 0)], writes=[(ssq, 0)])
                kb.op("dve", lambda e, ssk=ssk: e.tensor_scalar(
                    t(ssk)[:], t(ssk)[:], 1.0 / HD, EPS_RMS, op0=ALU.mult, op1=ALU.add),
                    reads=[(ssk, 0)], writes=[(ssk, 0)])
                kb.op("act", lambda e, ssq=ssq, lnt=lnt: e.activation(t(lnt)[:, 0:16], t(ssq)[:], AF.Ln),
                      reads=[(ssq, 0)], writes=[(lnt, 0)])
                kb.op("act", lambda e, ssk=ssk, lnt=lnt: e.activation(t(lnt)[:, 16:20], t(ssk)[:], AF.Ln),
                      reads=[(ssk, 0)], writes=[(lnt, 1)])
                kb.op("dve", lambda e, lnt=lnt: e.tensor_scalar_mul(t(lnt)[:], t(lnt)[:], -0.5),
                      reads=[(lnt, 0), (lnt, 1)], writes=[(lnt, 0), (lnt, 1)])
                kb.op("act", lambda e, lnt=lnt, rqraw=rqraw: e.activation(t(rqraw)[:], t(lnt)[:, 0:16], AF.Exp),
                      reads=[(lnt, 0)], writes=[(rqraw, 0)])
                kb.op("act", lambda e, lnt=lnt, rkraw=rkraw: e.activation(t(rkraw)[:], t(lnt)[:, 16:20], AF.Exp),
                      reads=[(lnt, 1)], writes=[(rkraw, 0)])
                kb.op("dve", lambda e, rqraw=rqraw, rstdq=rstdq: e.tensor_mul(
                    t(rstdq)[:], t(rqraw)[:], t("qgs")[:]),
                    reads=[(rqraw, 0), ("qgs", 0)], writes=[(rstdq, 0)])
                for h in range(16):
                    kb.op("dve", lambda e, h=h, qtok=qtok, rstdq=rstdq: e.tensor_scalar_mul(
                        t(qtok)[:, h * 128:(h + 1) * 128],
                        t(qtok)[:, h * 128:(h + 1) * 128], t(rstdq)[:, h:h + 1]),
                        reads=[(qtok, 0), (rstdq, 0)], writes=[(qtok, 0)])
                for h in range(4):
                    kb.op("dve", lambda e, h=h, ktok=ktok, rkraw=rkraw: e.tensor_scalar_mul(
                        t(ktok)[:, h * 128:(h + 1) * 128],
                        t(ktok)[:, h * 128:(h + 1) * 128], t(rkraw)[:, h:h + 1]),
                        reads=[(ktok, 0), (rkraw, 0)], writes=[(ktok, 0)])

                def rope(name, nh, tc=tc):
                    v1 = lambda: t(name)[:].rearrange("p (h d) -> p h d", d=128)[:, 0:nh, 0:8]
                    v2 = lambda: t(name)[:].rearrange("p (h d) -> p h d", d=128)[:, 0:nh, 8:16]
                    cv = lambda: t("cosb")[:, tc * 128:(tc + 1) * 128].rearrange(
                        "p (h j) -> p h j", j=8)[:, 0:nh, :]
                    sv = lambda: t("sinb")[:, tc * 128:(tc + 1) * 128].rearrange(
                        "p (h j) -> p h j", j=8)[:, 0:nh, :]
                    tv = lambda nm: t(nm)[:].rearrange("p (h j) -> p h j", j=8)[:, 0:nh, :]
                    key = (name, 0)
                    kb.op("dve", lambda e: e.tensor_mul(tv("tm1"), v1(), cv()),
                          reads=[key, ("cosb", 0)], writes=[("tm1", 0)])
                    kb.op("dve", lambda e: e.tensor_mul(tv("tm2"), v2(), sv()),
                          reads=[key, ("sinb", 0)], writes=[("tm2", 0)])
                    kb.op("dve", lambda e: e.tensor_mul(tv("tm3"), v1(), sv()),
                          reads=[key, ("sinb", 0)], writes=[("tm3", 0)])
                    kb.op("dve", lambda e: e.tensor_mul(tv("tm4"), v2(), cv()),
                          reads=[key, ("cosb", 0)], writes=[("tm4", 0)])
                    kb.op("dve", lambda e: e.tensor_add(v1(), tv("tm1"), tv("tm2")),
                          reads=[("tm1", 0), ("tm2", 0)], writes=[key])
                    kb.op("dve", lambda e: e.tensor_sub(v2(), tv("tm3"), tv("tm4")),
                          reads=[("tm3", 0), ("tm4", 0)], writes=[key])

                rope(qtok, 16)
                rope(ktok, 4)
                kb.op("dve", lambda e, qbf=qbf, qtok=qtok: e.tensor_copy(t(qbf)[:], t(qtok)[:]),
                      reads=[(qtok, 0)], writes=[(qbf, 0)])
                kb.op("dve", lambda e, kbf=kbf, ktok=ktok: e.tensor_copy(t(kbf)[:], t(ktok)[:]),
                      reads=[(ktok, 0)], writes=[(kbf, 0)])

            def s1_transposes(tc, seq=seq):
                bi = tc % 2
                qbf = f"qbf{bi}"
                kbf = f"kbf{bi}"
                qtc = f"qtc{bi}"
                for g in range(4):
                    bank = 6 + g % 2
                    for j in range(4):
                        hc = g * 4 + j
                        kb.op("pe", lambda e, hc=hc, bank=bank, j=j, qbf=qbf: e.matmul(
                            psv(bank, j * 128, (j + 1) * 128),
                            t(qbf)[:, hc * 128:(hc + 1) * 128], t("ident")[:],
                            start=True, stop=True),
                            reads=[(qbf, 0), ("ident", 0)], writes=[("ps", bank)])
                    kb.op("dve", lambda e, g=g, bank=bank, qtc=qtc: e.tensor_copy(
                        t(qtc)[:, g * 512:(g + 1) * 512], psv(bank)),
                        reads=[("ps", bank)], writes=[(qtc, 0)])
                for j in range(4):
                    kb.op("pe", lambda e, j=j, kbf=kbf: e.matmul(
                        psv(6, j * 128, (j + 1) * 128),
                        t(kbf)[:, j * 128:(j + 1) * 128], t("ident")[:],
                        start=True, stop=True),
                        reads=[(kbf, 0), ("ident", 0)], writes=[("ps", 6)])
                kb.op("dve", lambda e, tc=tc: e.tensor_copy(
                    t("kT")[:].rearrange("p (k n) -> p k n", k=4)[:, :, tc * 128:(tc + 1) * 128],
                    psv(6).rearrange("p (k n) -> p k n", k=4)),
                    reads=[("ps", 6)], writes=[("kTw", tc)])
                kb.op("act", lambda e, seq=seq, tc=tc, qtc=qtc: e.dma_start(
                    out=dram["qtd"][seq, tc], in_=t(qtc)[:]),
                    reads=[(qtc, 0)], writes=[("qtd", seq * 16 + tc)], dma=True)

            for tc in range(NTC):
                if 1 <= tc < NTC - 1:
                    xload_s1(tc + 1)
                s1_front(tc)
                if tc > 0:
                    s1_transposes(tc - 1)
            s1_transposes(NTC - 1)
        kb.close_scope(s1)

        s2 = f"s2_{seq}"
        kb.open_scope(s2)
        with kb.phase(s2):
            kb.sbuf("wp", [128, 16 * D], BF16, s2)
            for i in range(2):
                kb.sbuf(f"qt{i}", [128, D], BF16, s2)
                kb.sbuf(f"pt4_{i}", [128, 512], BF16, s2)
                kb.sbuf(f"res{i}", [128, D], F32, s2)
                kb.sbuf(f"x1bf{i}", [128, D], BF16, s2)
                kb.sbuf(f"xbf{i}", [128, D], BF16, s2)
            for i in range(3):
                kb.sbuf(f"diag4_{i}", [128, 512], BF16, s2)
                for h4 in range(4):
                    kb.sbuf(f"pbf{i}_{h4}", [128, T], BF16, s2)
            kb.sbuf("ytg", [128, 16 * 256], BF16, s2)

            for dc in range(16):
                kb.op("pool", lambda e, dc=dc: e.dma_start(
                    out=t("wp")[:, dc * D:(dc + 1) * D],
                    in_=dram["wpt"][dc * 128:(dc + 1) * 128, :]),
                    writes=[("wp", dc)], dma=True)
            if seq == 0:
                kb.emit_w_casts()

            def qload(qt, seq=seq):
                qb = f"qt{qt % 2}"
                kb.op("sync", lambda e, seq=seq, qt=qt, qb=qb: e.dma_start(
                    out=t(qb)[:], in_=dram["qtd"][seq, qt]),
                    reads=[("qtd", seq * 16 + qt)], writes=[(qb, 0)], dma=True)

            def xload(qt, seq=seq):
                rs = f"res{qt % 2}"
                kb.op("sync", lambda e, seq=seq, qt=qt, rs=rs: e.dma_start(
                    out=t(rs)[:], in_=dram["x"][seq, qt * 128:(qt + 1) * 128, :]),
                    writes=[(rs, c) for c in range(4)], dma=True)

            def softmax_g(qt, kvh):
                qb = f"qt{qt % 2}"
                span, nch, widths, off = score_geom(qt)
                c_last = nch - 1
                par = (4 * qt + kvh) % 3
                G = 4 if nch == 1 else (2 if nch == 2 else 1)

                def geom(h4):
                    base = h4 if nch == 1 else ((h4 % 2) * 2 if nch == 2 else 0)
                    return base, [("ps", base + c) for c in range(nch)]

                def ztail(hs):
                    for h4 in hs:
                        zp = f"zp{par}_{h4}"
                        kb.op("dve", lambda e, zp=zp, nch=nch, h4=h4, par=par:
                              e.reduce_sum(t(f"zz{par}_{h4}")[:], t(zp)[:, 0:nch],
                                           axis=AX.X),
                              reads=[(zp, c) for c in range(nch)],
                              writes=[(f"zz{par}_{h4}", 0)])
                    for h4 in hs:
                        kb.op("dve", lambda e, h4=h4, par=par: e.reciprocal(
                            t(f"rz{par}_{h4}")[:], t(f"zz{par}_{h4}")[:]),
                            reads=[(f"zz{par}_{h4}", 0)], writes=[(f"rz{par}_{h4}", 0)])
                    for h4 in hs:
                        kb.op("dve", lambda e, h4=h4, par=par: e.tensor_scalar_mul(
                            t(f"diag4_{par}")[:, h4 * 128:(h4 + 1) * 128],
                            t("ident")[:], t(f"rz{par}_{h4}")[:]),
                            reads=[("ident", 0), (f"rz{par}_{h4}", 0)],
                            writes=[(f"diag4_{par}", h4)])

                prev_hs = None
                for g0 in range(0, 4, G):
                    hs = list(range(g0, g0 + G))
                    for h4 in hs:
                        h = kvh * 4 + h4
                        base, bank_keys = geom(h4)
                        for c in range(nch):
                            last = (c == nch - 1)
                            kb.op("pe", lambda e, qb=qb, h=h, kvh=kvh, c=c,
                                  w=widths[c], base=base, last=last:
                                  e.matmul(psv(base + c, 0, w),
                                           t(qb)[:, h * 128:(h + 1) * 128],
                                           t("kT")[:, kvh * T + c * 512: kvh * T + c * 512 + w],
                                           start=True, stop=not last),
                                  reads=[(qb, 0)] + [("kTw", i) for i in
                                         range((c * 512) // 128,
                                               min(NTC, (c * 512 + widths[c] + 127) // 128))],
                                  writes=[bank_keys[c]])
                        kb.op("pe", lambda e, base=base, c=c_last, off=widths[c_last] - 128:
                              e.matmul(psv(base + c, off, off + 128),
                                       t("maskb")[:], t("ident")[:],
                                       start=False, stop=True),
                              reads=[("maskb", 0), ("ident", 0)],
                              writes=[bank_keys[-1]])
                        yield
                    for h4 in hs:
                        base, bank_keys = geom(h4)
                        cm = f"cmx{par}_{h4}"
                        for c in range(nch):
                            kb.op("dve", lambda e, base=base, c=c, w=widths[c], cm=cm:
                                  e.reduce_max(t(cm)[:, c:c + 1], psv(base + c, 0, w),
                                               axis=AX.X),
                                  reads=[bank_keys[c]], writes=[(cm, c)])
                    yield
                    for h4 in hs:
                        cm = f"cmx{par}_{h4}"
                        kb.op("dve", lambda e, cm=cm, nch=nch, h4=h4, par=par:
                              e.reduce_max(t(f"nm{par}_{h4}")[:], t(cm)[:, 0:nch],
                                           axis=AX.X, negate=True),
                              reads=[(cm, c) for c in range(nch)],
                              writes=[(f"nm{par}_{h4}", 0)])
                    yield
                    if prev_hs is not None:
                        ztail(prev_hs)
                        yield
                    for h4 in hs:
                        base, bank_keys = geom(h4)
                        pb = f"pbf{par}_{h4}"
                        zp = f"zp{par}_{h4}"
                        for c in range(nch):
                            kb.op("act", lambda e, base=base, c=c, w=widths[c],
                                  pb=pb, zp=zp, h4=h4, par=par:
                                  e.activation(
                                      t(pb)[:, c * 512: c * 512 + w],
                                      psv(base + c, 0, w), AF.Exp,
                                      bias=t(f"nm{par}_{h4}")[:], scale=1.0,
                                      accum_out=t(zp)[:, c:c + 1]),
                                  reads=[bank_keys[c], (f"nm{par}_{h4}", 0)],
                                  writes=[(pb, c), (zp, c)])
                        yield
                    prev_hs = hs
                ztail(prev_hs)
                yield

            def ptpv_g(qt, kvh):
                tlq = qt % 2
                par = (4 * qt + kvh) % 3
                ob = 6 + kvh % 2

                def trans(sblk):
                    ptb = 4 + sblk % 2
                    for h4 in range(4):
                        kb.op("pe", lambda e, h4=h4, sblk=sblk, ptb=ptb, par=par:
                              e.matmul(
                                  psv(ptb, h4 * 128, (h4 + 1) * 128),
                                  t(f"pbf{par}_{h4}")[:, sblk * 128:(sblk + 1) * 128],
                                  t(f"diag4_{par}")[:, h4 * 128:(h4 + 1) * 128],
                                  start=True, stop=True),
                              reads=[(f"pbf{par}_{h4}", (sblk * 128) // 512),
                                     (f"diag4_{par}", h4)],
                              writes=[("ps", ptb)])

                def cast(sblk):
                    ptb = 4 + sblk % 2
                    pt = f"pt4_{sblk % 2}"
                    if sblk % 2 == 0:
                        kb.op("dve", lambda e, ptb=ptb, pt=pt: e.tensor_copy(
                            t(pt)[:], psv(ptb)),
                            reads=[("ps", ptb)], writes=[(pt, 0)])
                    else:
                        kb.op("act", lambda e, ptb=ptb, pt=pt: e.activation(
                            t(pt)[:], psv(ptb), AF.Copy),
                            reads=[("ps", ptb)], writes=[(pt, 0)])

                def pv(sblk):
                    pt = f"pt4_{sblk % 2}"
                    kb.op("pe", lambda e, sblk=sblk, kvh=kvh, pt=pt, qt=qt, ob=ob:
                          e.matmul(
                              psv(ob),
                              t("vres")[:, sblk * KD + kvh * 128: sblk * KD + (kvh + 1) * 128],
                              t(pt)[:], start=(sblk == 0), stop=(sblk == qt)),
                          reads=[("vres", sblk), (pt, 0)], writes=[("ps", ob)])

                for sblk in range(qt + 1):
                    trans(sblk)
                    if sblk >= 1:
                        cast(sblk - 1)
                    if sblk >= 2:
                        pv(sblk - 2)
                    yield
                cast(qt)
                if qt >= 1:
                    pv(qt - 1)
                pv(qt)
                kb.op("act", lambda e, kvh=kvh, tlq=tlq, ob=ob: e.activation(
                    t("ytg")[:].rearrange("p (d x) -> p d x", x=256)[
                        :, kvh * 4:(kvh + 1) * 4, tlq * 128:(tlq + 1) * 128],
                    psv(ob).rearrange("p (d x) -> p d x", x=128),
                    AF.Copy),
                    reads=[("ps", ob)], writes=[("ytg", (tlq, kvh))])

            def ln_steps(insts):
                for c4 in range(4):
                    for rs_name, out_name, li in insts:
                        kb.op("dve", lambda e, c4=c4, rs_name=rs_name, li=li: e.bn_stats(
                            t(f"bn24_{li}")[:, c4 * 6:(c4 + 1) * 6],
                            t(rs_name)[:, c4 * 512:(c4 + 1) * 512]),
                            reads=[(rs_name, c4)],
                            writes=[(f"bn24_{li}", 0)])
                for rs_name, out_name, li in insts:
                    kb.op("dve", lambda e, li=li: e.reduce_sum(
                        t(f"summ_{li}")[:],
                        t(f"bn24_{li}")[:].rearrange("p (g s) -> p s g", s=3)[:, 1, :],
                        axis=AX.X),
                        reads=[(f"bn24_{li}", 0)], writes=[(f"summ_{li}", 0)])
                    kb.op("dve", lambda e, li=li: e.reduce_sum(
                        t(f"sM2_{li}")[:],
                        t(f"bn24_{li}")[:].rearrange("p (g s) -> p s g", s=3)[:, 2, :],
                        axis=AX.X),
                        reads=[(f"bn24_{li}", 0)], writes=[(f"sM2_{li}", 0)])
                for rs_name, out_name, li in insts:
                    kb.op("dve", lambda e, li=li: e.tensor_scalar_mul(
                        t(f"meant_{li}")[:], t(f"summ_{li}")[:], 0.125),
                        reads=[(f"summ_{li}", 0)], writes=[(f"meant_{li}", 0)])
                    kb.op("dve", lambda e, li=li: e.tensor_scalar(
                        t(f"dev_{li}")[:],
                        t(f"bn24_{li}")[:].rearrange("p (g s) -> p s g", s=3)[:, 1, :],
                        t(f"meant_{li}")[:], None, op0=ALU.subtract),
                        reads=[(f"bn24_{li}", 0), (f"meant_{li}", 0)],
                        writes=[(f"dev_{li}", 0)])
                for rs_name, out_name, li in insts:
                    kb.op("dve", lambda e, li=li: e.tensor_mul(
                        t(f"dev2_{li}")[:], t(f"dev_{li}")[:], t(f"dev_{li}")[:]),
                        reads=[(f"dev_{li}", 0)], writes=[(f"dev2_{li}", 0)])
                for rs_name, out_name, li in insts:
                    kb.op("dve", lambda e, li=li: e.reduce_sum(
                        t(f"sdev_{li}")[:], t(f"dev2_{li}")[:], axis=AX.X),
                        reads=[(f"dev2_{li}", 0)], writes=[(f"sdev_{li}", 0)])
                    kb.op("dve", lambda e, li=li: e.tensor_scalar(
                        t(f"sM2_{li}")[:], t(f"sM2_{li}")[:], 1.0 / D, EPS_LN,
                        op0=ALU.mult, op1=ALU.add),
                        reads=[(f"sM2_{li}", 0)], writes=[(f"sM2_{li}", 0)])
                for rs_name, out_name, li in insts:
                    kb.op("dve", lambda e, li=li: e.scalar_tensor_tensor(
                        t(f"vart_{li}")[:], t(f"sdev_{li}")[:], 0.125, t(f"sM2_{li}")[:],
                        op0=ALU.mult, op1=ALU.add),
                        reads=[(f"sdev_{li}", 0), (f"sM2_{li}", 0)],
                        writes=[(f"vart_{li}", 0)])
                for rs_name, out_name, li in insts:
                    kb.op("act", lambda e, li=li: e.activation(
                        t(f"lnv_{li}")[:], t(f"vart_{li}")[:], AF.Ln),
                        reads=[(f"vart_{li}", 0)], writes=[(f"lnv_{li}", 0)])
                for rs_name, out_name, li in insts:
                    kb.op("dve", lambda e, li=li: e.tensor_scalar_mul(
                        t(f"lnv_{li}")[:], t(f"lnv_{li}")[:], -0.5),
                        reads=[(f"lnv_{li}", 0)], writes=[(f"lnv_{li}", 0)])
                for rs_name, out_name, li in insts:
                    kb.op("act", lambda e, li=li: e.activation(
                        t(f"rstd1_{li}")[:], t(f"lnv_{li}")[:], AF.Exp),
                        reads=[(f"lnv_{li}", 0)], writes=[(f"rstd1_{li}", 0)])
                for rs_name, out_name, li in insts:
                    kb.op("dve", lambda e, li=li: e.tensor_scalar(
                        t(f"nmr_{li}")[:], t(f"meant_{li}")[:], t(f"rstd1_{li}")[:], -1.0,
                        op0=ALU.mult, op1=ALU.mult),
                        reads=[(f"meant_{li}", 0), (f"rstd1_{li}", 0)],
                        writes=[(f"nmr_{li}", 0)])
                for rs_name, out_name, li in insts:
                    kb.op("act", lambda e, rs_name=rs_name, out_name=out_name, li=li:
                          e.activation(
                              t(out_name)[:], t(rs_name)[:], AF.Identity,
                              bias=t(f"nmr_{li}")[:], scale=t(f"rstd1_{li}")[:]),
                          reads=[(rs_name, i) for i in range(4)] +
                                [(f"nmr_{li}", 0), (f"rstd1_{li}", 0)],
                          writes=[(out_name, 0)])

            def proj_ln_store_g(qt, seq=seq):
                tl = qt % 2
                rs = f"res{qt % 2}"
                x1 = f"x1bf{qt % 2}"
                xb = f"xbf{qt % 2}"
                li = qt % 2
                kb.op("act", lambda e, rs=rs, xb=xb: e.activation(
                    t(xb)[:], t(rs)[:], AF.Copy),
                    reads=[(rs, c) for c in range(4)], writes=[(xb, 0)])
                def evac(q):
                    bank = 4 + q % 2
                    kb.op("act", lambda e, q=q, bank=bank, rs=rs: e.activation(
                        t(rs)[:, q * 512:(q + 1) * 512], psv(bank), AF.Copy),
                        reads=[("ps", bank)], writes=[(rs, q)])

                for q in range(4):
                    bank = 4 + q % 2
                    kb.op("pe", lambda e, q=q, bank=bank, xb=xb: e.matmul(
                        psv(bank), t("ident")[:],
                        t(xb)[:, q * 512:(q + 1) * 512],
                        start=True, stop=False),
                        reads=[("ident", 0), (xb, 0)], writes=[("ps", bank)])
                    for dc in range(16):
                        kb.op("pe", lambda e, dc=dc, q=q, tl=tl, bank=bank: e.matmul(
                            psv(bank),
                            t("ytg")[:, dc * 256 + tl * 128: dc * 256 + (tl + 1) * 128],
                            t("wp")[:, dc * D + q * 512: dc * D + (q + 1) * 512],
                            start=False, stop=(dc == 15)),
                            reads=[("ytg", (tl, k)) for k in range(4)] + [("wp", dc)],
                            writes=[("ps", bank)])
                    if q >= 1:
                        evac(q - 1)
                    yield
                evac(3)
                ln_steps([(rs, x1, li)])
                kb.op("act", lambda e, seq=seq, qt=qt, x1=x1: e.dma_start(
                    out=dram["x1k"][seq, qt], in_=t(x1)[:]),
                    reads=[(x1, 0)], writes=[("x1k", seq * 16 + qt)], dma=True)

            def pull(Q):
                try:
                    next(Q[0][1])
                    return True
                except StopIteration:
                    Q.pop(0)
                    return False

            def interleave(sm_gen, Q, forbid=None):
                if forbid is not None:
                    while any(tag == forbid for tag, _ in Q):
                        pull(Q)
                for _ in sm_gen:
                    pulls = 2
                    while pulls > 0 and Q:
                        if pull(Q):
                            pulls -= 1

            def drain(Q, upto=None):
                while Q:
                    tag = Q[0][0]
                    if not pull(Q):
                        if upto is not None and tag == upto:
                            return

            qload(0)
            qload(1)
            xload(0)
            Q = []
            for qt in range(NTC):
                for kvh in range(4):
                    u = 4 * qt + kvh
                    if kvh == 1 and qt > 0:
                        Q.append(("proj", proj_ln_store_g(qt - 1)))
                    interleave(softmax_g(qt, kvh), Q, forbid=u % 3)
                    Q.append((u % 3, ptpv_g(qt, kvh)))
                    if kvh == 2:
                        if qt > 0:
                            drain(Q, upto="proj")
                        if qt + 1 < NTC:
                            xload(qt + 1)
                if qt + 2 < NTC:
                    qload(qt + 2)
            drain(Q)
            drain([("proj", proj_ln_store_g(NTC - 1))])
        kb.close_scope(s2)
        kb.close_scope(sc)

    kb.open_scope("s3")
    with kb.phase("s3"):
        kb.sbuf("x1tg0", [128, 4 * D], BF16, "s3")
        for i in range(4):
            kb.sbuf(f"w1h{i}", [128, 8 * 128], BF16, "s3")
            kb.sbuf(f"w3h{i}", [128, 16 * 128], BF16, "s3")
        for i in range(6):
            kb.sbuf(f"w2h{i}", [128, 16 * 128], BF16, "s3")
        for i in range(3):
            kb.sbuf(f"otbf{i}", [128, 512], BF16, "s3")
        kb.sbuf("x1ks0", [128, 4 * D], BF16, "s3")
        kb.sbuf("x1ks1", [128, 4 * D], BF16, "s3")
        kb.sbuf("h1", [128, 32 * 512], BF16, "s3")
        kb.sbuf("h2", [128, 32 * 512], BF16, "s3")
        for i in range(4):
            kb.sbuf(f"res2_{i}", [128, D], F32, "s3")

        def x1_load_transpose(gidx):
            seq, g = divmod(gidx, 4)
            xk = f"x1ks{gidx % 2}"
            xg = "x1tg0"
            for tcl in range(4):
                kb.op("sync", lambda e, seq=seq, g=g, tcl=tcl, xk=xk: e.dma_start(
                    out=t(xk)[:, tcl * D:(tcl + 1) * D],
                    in_=dram["x1k"][seq, g * 4 + tcl]),
                    reads=[("x1k", seq * 16 + g * 4 + tcl)], writes=[(xk, tcl)], dma=True)
            for tcl in range(4):
                for g2 in range(4):
                    bank = 6 + (tcl * 4 + g2) % 2
                    for j in range(4):
                        dc = g2 * 4 + j
                        kb.op("pe", lambda e, tcl=tcl, dc=dc, j=j, bank=bank, xk=xk: e.matmul(
                            psv(bank, j * 128, (j + 1) * 128),
                            t(xk)[:, tcl * D + dc * 128: tcl * D + (dc + 1) * 128],
                            t("ident")[:], start=True, stop=True),
                            reads=[(xk, tcl), ("ident", 0)], writes=[("ps", bank)])
                    kb.op("dve", lambda e, tcl=tcl, g2=g2, bank=bank, xg=xg: e.tensor_copy(
                        t(xg)[:, tcl * D + g2 * 512: tcl * D + (g2 + 1) * 512], psv(bank)),
                        reads=[("ps", bank)], writes=[(xg, tcl)])

        for gidx in range(8):
            seq, g = divmod(gidx, 4)
            xg = "x1tg0"
            xk = f"x1ks{gidx % 2}"
            if gidx == 0:
                x1_load_transpose(0)

            def xg_view(dc, xg=xg):
                return t(xg)[:].rearrange("p (c n) -> p c n", c=4)[:, :, dc * 128:(dc + 1) * 128]

            RING4 = ("sync", "pool")
            for hc in range(32):
                for hf in range(2):
                    idx = 2 * hc + hf
                    ws = f"w1h{idx % 4}"
                    kb.op(RING4[idx % 2], lambda e, hc=hc, hf=hf, ws=ws: e.dma_start(
                        out=t(ws)[:],
                        in_=dram["w1d"][hc][:, hf * 8:(hf + 1) * 8, :].rearrange(
                            "p c n -> p (c n)")),
                        reads=[("w1d", hc // 16)], writes=[(ws, 0)], dma=True)
                bank = hc % 4
                for dc in range(16):
                    ws = f"w1h{(2 * hc + dc // 8) % 4}"
                    kb.op("pe", lambda e, dc=dc, ws=ws, bank=bank, xg=xg: e.matmul(
                        psv(bank), t(ws)[:, (dc % 8) * 128:(dc % 8 + 1) * 128],
                        xg_view(dc, xg), start=(dc == 0), stop=(dc == 15)),
                        reads=[(ws, 0)] + [(xg, i) for i in range(4)],
                        writes=[("ps", bank)])
                kb.op("act", lambda e, hc=hc, bank=bank: e.activation(
                    t("h1")[:, hc * 512:(hc + 1) * 512], psv(bank), AF.Relu,
                    bias=t("b1c")[:, hc:hc + 1], scale=1.0),
                    reads=[("ps", bank), ("b1c", 0)], writes=[("h1", hc)])
            for jc in range(32):
                for hf in range(2):
                    idx = 2 * jc + hf
                    ws = f"w2h{idx % 6}"
                    kb.op(RING4[idx % 2], lambda e, jc=jc, hf=hf, ws=ws: e.dma_start(
                        out=t(ws)[:],
                        in_=dram["w2d"][jc][:, hf * 16:(hf + 1) * 16, :].rearrange(
                            "p c n -> p (c n)")),
                        reads=[("w2d", jc // 8)], writes=[(ws, 0)], dma=True)
                bank = jc % 4
                for hc in range(32):
                    ws = f"w2h{(2 * jc + hc // 16) % 6}"
                    kb.op("pe", lambda e, hc=hc, ws=ws, bank=bank: e.matmul(
                        psv(bank), t(ws)[:, (hc % 16) * 128:(hc % 16 + 1) * 128],
                        t("h1")[:, hc * 512:(hc + 1) * 512],
                        start=(hc == 0), stop=(hc == 31)),
                        reads=[(ws, 0), ("h1", hc)], writes=[("ps", bank)])
                kb.op("act", lambda e, jc=jc, bank=bank: e.activation(
                    t("h2")[:, jc * 512:(jc + 1) * 512], psv(bank), AF.Relu,
                    bias=t("b2c")[:, jc:jc + 1], scale=1.0),
                    reads=[("ps", bank), ("b2c", 0)], writes=[("h2", jc)])
            if gidx + 1 < 8:
                x1_load_transpose(gidx + 1)
            for oc in range(16):
                ob = f"otbf{oc % 2}"
                for hf in range(2):
                    idx = 2 * oc + hf
                    ws = f"w3h{idx % 4}"
                    kb.op(RING4[idx % 2], lambda e, oc=oc, hf=hf, ws=ws: e.dma_start(
                        out=t(ws)[:],
                        in_=dram["w3d"][oc][:, hf * 16:(hf + 1) * 16, :].rearrange(
                            "p c n -> p (c n)")),
                        reads=[("w3d", oc // 8)], writes=[(ws, 0)], dma=True)
                bank = oc % 2
                for hc in range(32):
                    ws = f"w3h{(2 * oc + hc // 16) % 4}"
                    kb.op("pe", lambda e, hc=hc, ws=ws, bank=bank: e.matmul(
                        psv(bank), t(ws)[:, (hc % 16) * 128:(hc % 16 + 1) * 128],
                        t("h2")[:, hc * 512:(hc + 1) * 512],
                        start=(hc == 0), stop=(hc == 31)),
                        reads=[(ws, 0), ("h2", hc)], writes=[("ps", bank)])
                kb.op("act", lambda e, oc=oc, bank=bank, ob=ob: e.activation(
                    t(ob)[:], psv(bank), AF.Identity,
                    bias=t("b3c")[:, oc:oc + 1], scale=1.0),
                    reads=[("ps", bank), ("b3c", 0)], writes=[(ob, 0)])
                tbank = 4 + oc % 2
                for tl in range(4):
                    kb.op("pe", lambda e, tl=tl, ob=ob, tbank=tbank: e.matmul(
                        psv(tbank, tl * 128, (tl + 1) * 128),
                        t(ob)[:, tl * 128:(tl + 1) * 128], t("ident")[:],
                        start=True, stop=True),
                        reads=[(ob, 0), ("ident", 0)], writes=[("ps", tbank)])
                for tl in range(4):
                    kb.op("dve", lambda e, tl=tl, oc=oc, tbank=tbank, xk=xk: e.tensor_add(
                        t(f"res2_{tl}")[:, oc * 128:(oc + 1) * 128],
                        psv(tbank, tl * 128, (tl + 1) * 128),
                        t(xk)[:, tl * D + oc * 128: tl * D + (oc + 1) * 128]),
                        reads=[("ps", tbank), (xk, tl)],
                        writes=[(f"res2_{tl}", oc), (f"res2_{tl}", "ln")])
            insts = [(f"res2_{tl}", None, tl) for tl in range(4)]
            for c4 in range(4):
                for rs_name, out_name, li in insts:
                    kb.op("dve", lambda e, c4=c4, rs_name=rs_name, li=li: e.bn_stats(
                        t(f"bn24_{li}")[:, c4 * 6:(c4 + 1) * 6],
                        t(rs_name)[:, c4 * 512:(c4 + 1) * 512]),
                        reads=[(rs_name, i) for i in range(16)], writes=[(f"bn24_{li}", 0)])
            for rs_name, out_name, li in insts:
                kb.op("dve", lambda e, li=li: e.reduce_sum(
                    t(f"summ_{li}")[:],
                    t(f"bn24_{li}")[:].rearrange("p (g s) -> p s g", s=3)[:, 1, :],
                    axis=AX.X),
                    reads=[(f"bn24_{li}", 0)], writes=[(f"summ_{li}", 0)])
                kb.op("dve", lambda e, li=li: e.reduce_sum(
                    t(f"sM2_{li}")[:],
                    t(f"bn24_{li}")[:].rearrange("p (g s) -> p s g", s=3)[:, 2, :],
                    axis=AX.X),
                    reads=[(f"bn24_{li}", 0)], writes=[(f"sM2_{li}", 0)])
            for rs_name, out_name, li in insts:
                kb.op("dve", lambda e, li=li: e.tensor_scalar_mul(
                    t(f"meant_{li}")[:], t(f"summ_{li}")[:], 0.125),
                    reads=[(f"summ_{li}", 0)], writes=[(f"meant_{li}", 0)])
                kb.op("dve", lambda e, li=li: e.tensor_scalar(
                    t(f"dev_{li}")[:],
                    t(f"bn24_{li}")[:].rearrange("p (g s) -> p s g", s=3)[:, 1, :],
                    t(f"meant_{li}")[:], None, op0=ALU.subtract),
                    reads=[(f"bn24_{li}", 0), (f"meant_{li}", 0)], writes=[(f"dev_{li}", 0)])
            for rs_name, out_name, li in insts:
                kb.op("dve", lambda e, li=li: e.tensor_mul(
                    t(f"dev2_{li}")[:], t(f"dev_{li}")[:], t(f"dev_{li}")[:]),
                    reads=[(f"dev_{li}", 0)], writes=[(f"dev2_{li}", 0)])
                kb.op("dve", lambda e, li=li: e.reduce_sum(
                    t(f"sdev_{li}")[:], t(f"dev2_{li}")[:], axis=AX.X),
                    reads=[(f"dev2_{li}", 0)], writes=[(f"sdev_{li}", 0)])
                kb.op("dve", lambda e, li=li: e.tensor_scalar(
                    t(f"sM2_{li}")[:], t(f"sM2_{li}")[:], 1.0 / D, EPS_LN,
                    op0=ALU.mult, op1=ALU.add),
                    reads=[(f"sM2_{li}", 0)], writes=[(f"sM2_{li}", 0)])
            for rs_name, out_name, li in insts:
                kb.op("dve", lambda e, li=li: e.scalar_tensor_tensor(
                    t(f"vart_{li}")[:], t(f"sdev_{li}")[:], 0.125, t(f"sM2_{li}")[:],
                    op0=ALU.mult, op1=ALU.add),
                    reads=[(f"sdev_{li}", 0), (f"sM2_{li}", 0)], writes=[(f"vart_{li}", 0)])
            for rs_name, out_name, li in insts:
                kb.op("act", lambda e, li=li: e.activation(
                    t(f"lnv_{li}")[:], t(f"vart_{li}")[:], AF.Ln),
                    reads=[(f"vart_{li}", 0)], writes=[(f"lnv_{li}", 0)])
            for rs_name, out_name, li in insts:
                kb.op("dve", lambda e, li=li: e.tensor_scalar_mul(
                    t(f"lnv_{li}")[:], t(f"lnv_{li}")[:], -0.5),
                    reads=[(f"lnv_{li}", 0)], writes=[(f"lnv_{li}", 0)])
            for rs_name, out_name, li in insts:
                kb.op("act", lambda e, li=li: e.activation(
                    t(f"rstd1_{li}")[:], t(f"lnv_{li}")[:], AF.Exp),
                    reads=[(f"lnv_{li}", 0)], writes=[(f"rstd1_{li}", 0)])
            for rs_name, out_name, li in insts:
                kb.op("dve", lambda e, li=li: e.tensor_scalar(
                    t(f"nmr_{li}")[:], t(f"meant_{li}")[:], t(f"rstd1_{li}")[:], -1.0,
                    op0=ALU.mult, op1=ALU.mult),
                    reads=[(f"meant_{li}", 0), (f"rstd1_{li}", 0)], writes=[(f"nmr_{li}", 0)])
            for rs_name, out_name, li in insts:
                tl = li
                kb.op("act", lambda e, rs_name=rs_name, li=li: e.activation(
                    t(rs_name)[:], t(rs_name)[:], AF.Identity,
                    bias=t(f"nmr_{li}")[:], scale=t(f"rstd1_{li}")[:]),
                    reads=[(rs_name, i) for i in range(16)] +
                          [(f"nmr_{li}", 0), (f"rstd1_{li}", 0)],
                    writes=[(rs_name, "ln")])
                kb.op("act", lambda e, seq=seq, g=g, tl=tl, rs_name=rs_name: e.dma_start(
                    out=dram["y"][seq, (g * 4 + tl) * 128:(g * 4 + tl + 1) * 128, :],
                    in_=t(rs_name)[:]),
                    reads=[(rs_name, "ln")], writes=[("ydone", gidx * 4 + tl)], dma=True)
    kb.close_scope("s3")

    with kb.phase("fin"):
        def fin(e):
            tot = kb.totals()
            for semname, val in tot.items():
                if val > 0 and not semname.startswith("qsync"):
                    e.wait_ge(kb.sems[semname], val)
            for semname, val in tot.items():
                if val > 0 and semname.startswith("qsync"):
                    e.wait_ge(kb.sems[semname], val)
            return None
        kb.op("sync", fin)
    kb.close_scope("glob")


def _host_prep(inputs):
    f32 = np.float32
    x = np.ascontiguousarray(inputs["L0_attn_x"], dtype=f32)
    wq = np.asarray(inputs["L0_attn_wq"], dtype=f32)
    wk = np.asarray(inputs["L0_attn_wk"], dtype=f32)
    wv = np.asarray(inputs["L0_attn_wv"], dtype=f32)
    wp = np.asarray(inputs["L0_attn_wp"], dtype=f32)
    qg = np.asarray(inputs["L0_attn_qg"], dtype=f32)
    W1 = np.asarray(inputs["L0_ffn_W1"], dtype=f32)
    b1 = np.asarray(inputs["L0_ffn_b1"], dtype=f32)
    W2 = np.asarray(inputs["L0_ffn_W2"], dtype=f32)
    b2 = np.asarray(inputs["L0_ffn_b2"], dtype=f32)
    W3 = np.asarray(inputs["L0_ffn_W3"], dtype=f32)
    b3 = np.asarray(inputs["L0_ffn_b3"], dtype=f32)

    shared = {}
    shared["wqt"] = np.ascontiguousarray(wq.T)
    shared["wkt"] = np.ascontiguousarray(wk.T)
    shared["wvt"] = np.ascontiguousarray(wv.T)
    shared["wpt"] = np.ascontiguousarray(wp.T)
    W1T = W1.T
    W2T = W2.T
    W3T = W3.T
    shared["w1b"] = np.ascontiguousarray(
        W1T.reshape(16, 128, 32, 128).transpose(2, 1, 0, 3))
    shared["w2b"] = np.ascontiguousarray(
        W2T.reshape(32, 128, 32, 128).transpose(2, 1, 0, 3))
    shared["w3b"] = np.ascontiguousarray(
        W3T.reshape(32, 128, 16, 128).transpose(2, 1, 0, 3))
    shared["b1cd"] = np.ascontiguousarray(b1.reshape(32, 128).T)
    shared["b2cd"] = np.ascontiguousarray(b2.reshape(32, 128).T)
    shared["b3cd"] = np.ascontiguousarray(b3.reshape(16, 128).T)
    shared["qgi"] = np.ascontiguousarray(qg.reshape(1, 16))
    shared["identi"] = np.eye(128, dtype=ml_dtypes.bfloat16)
    shared["maski"] = np.triu(np.full((128, 128), NEG_BIG, dtype=f32), k=1)
    shared["masktb"] = np.ascontiguousarray(
        np.tril(np.full((128, 128), NEG_BIG), k=-1)).astype(ml_dtypes.bfloat16)
    inv = (1.0 / 10000.0 ** (np.arange(0, PD, 2, dtype=f32) / PD)).astype(f32)
    f = np.arange(T, dtype=f32)[:, None] * inv[None, :]
    cosf = np.cos(f).astype(f32)
    sinf = np.sin(f).astype(f32)
    cost = np.tile(cosf, (1, 16))
    sint = np.tile(sinf, (1, 16))
    shared["cosbd"] = np.ascontiguousarray(
        cost.reshape(16, 128, 128).transpose(1, 0, 2).reshape(128, T))
    shared["sinbd"] = np.ascontiguousarray(
        sint.reshape(16, 128, 128).transpose(1, 0, 2).reshape(128, T))

    in_maps = []
    for c in range(NCORES):
        xs = x[2 * c:2 * c + 2]
        m = dict(shared)
        m["x"] = xs
        m["xtb"] = np.ascontiguousarray(
            xs.reshape(2, 16, 128, 16, 128).transpose(0, 1, 4, 3, 2).reshape(
                2, 16, 128, 2048))
        in_maps.append(m)
    return in_maps


def _build_program():
    nc = bass.Bass("TRN2", target_bir_lowering=False, debug=False)
    dram = {}
    dram["x"] = nc.dram_tensor("x", [2, T, D], F32, kind="ExternalInput").ap()
    dram["xtb"] = nc.dram_tensor("xtb", [2, NTC, 128, D], F32, kind="ExternalInput").ap()
    dram["wqt"] = nc.dram_tensor("wqt", [D, D], F32, kind="ExternalInput").ap()
    dram["wkt"] = nc.dram_tensor("wkt", [D, KD], F32, kind="ExternalInput").ap()
    dram["wvt"] = nc.dram_tensor("wvt", [D, KD], F32, kind="ExternalInput").ap()
    dram["wpt"] = nc.dram_tensor("wpt", [D, D], F32, kind="ExternalInput").ap()
    dram["w1b"] = nc.dram_tensor("w1b", [32, 128, 16, 128], F32, kind="ExternalInput").ap()
    dram["w2b"] = nc.dram_tensor("w2b", [32, 128, 32, 128], F32, kind="ExternalInput").ap()
    dram["w3b"] = nc.dram_tensor("w3b", [16, 128, 32, 128], F32, kind="ExternalInput").ap()
    dram["b1c"] = nc.dram_tensor("b1cd", [128, 32], F32, kind="ExternalInput").ap()
    dram["b2c"] = nc.dram_tensor("b2cd", [128, 32], F32, kind="ExternalInput").ap()
    dram["b3c"] = nc.dram_tensor("b3cd", [128, 16], F32, kind="ExternalInput").ap()
    dram["qgi"] = nc.dram_tensor("qgi", [1, 16], F32, kind="ExternalInput").ap()
    dram["identi"] = nc.dram_tensor("identi", [128, 128], BF16, kind="ExternalInput").ap()
    dram["maski"] = nc.dram_tensor("maski", [128, 128], F32, kind="ExternalInput").ap()
    dram["masktb"] = nc.dram_tensor("masktb", [128, 128], BF16, kind="ExternalInput").ap()
    dram["cosb"] = nc.dram_tensor("cosbd", [128, T], F32, kind="ExternalInput").ap()
    dram["sinb"] = nc.dram_tensor("sinbd", [128, T], F32, kind="ExternalInput").ap()
    dram["qtd"] = nc.dram_tensor("qtd", [2, NTC, 128, D], BF16, kind="Internal").ap()
    dram["x1t"] = nc.dram_tensor("x1td", [2, NTC, 128, D], BF16, kind="Internal").ap()
    dram["x1k"] = nc.dram_tensor("x1kd", [2, NTC, 128, D], BF16, kind="Internal").ap()
    dram["wpd"] = nc.dram_tensor("wpd", [D, D], BF16, kind="Internal").ap()
    dram["w1d"] = nc.dram_tensor("w1d", [32, 128, 16, 128], BF16, kind="Internal").ap()
    dram["w2d"] = nc.dram_tensor("w2d", [32, 128, 32, 128], BF16, kind="Internal").ap()
    dram["w3d"] = nc.dram_tensor("w3d", [16, 128, 32, 128], BF16, kind="Internal").ap()
    dram["y"] = nc.dram_tensor("y", [2, T, D], F32, kind="ExternalOutput").ap()

    kb = KB(nc)
    build_all(kb, dram)
    kb.finish_plan()
    with ExitStack() as st:
        kb.make_sems(st)
        build_all(kb, dram)
    for s in KB.STREAMS:
        assert kb.pos[s] == len(kb.ops[s]), (s, kb.pos[s], len(kb.ops[s]))
    return nc


def kernel(**inputs):
    in_maps = _host_prep(inputs)
    nc = _build_program()
    res = run_bass_kernel_spmd(nc, in_maps, core_ids=list(range(NCORES)))
    outs = [np.asarray(r["y"], dtype=np.float32) for r in res.results]
    return np.concatenate(outs, axis=0)


if __name__ == "__main__":
    nc = bass.Bass("TRN2", target_bir_lowering=False)
    dram = {k: None for k in ()}
    kb = KB(nc)

    class _D(dict):
        def __getitem__(self, k):
            return FakeAP()

    class FakeAP:
        shape = (32, 128, 16, 128)

        def __getitem__(self, *_):
            return self

        def rearrange(self, *a, **k):
            return self

    build_all(kb, _D())
    for s in KB.STREAMS:
        nd = sum(1 for r in kb.ops[s] if r.dma)
        nn = sum(1 for r in kb.ops[s] if r.needed)
        print(f"{s}: ops={len(kb.ops[s])} dma={nd} milestones={nn}")
    print("sbuf bytes/partition by scope:", {k: v for k, v in kb.sbuf_bytes.items()})
